# version 04
# baseline (speedup 1.0000x reference)
import math
from collections import defaultdict
from contextlib import ExitStack

import numpy as np
import ml_dtypes

import concourse.bass as bass
import concourse.mybir as mybir
from concourse.bass_utils import run_bass_kernel_spmd

F32 = mybir.dt.float32
F16 = mybir.dt.float16
AF = mybir.ActivationFunctionType
OP = mybir.AluOpType

NCORES = 8
E = 1024
FF = 4096
H = 16
DH = 64
M = 256
CH = 128
LN16 = float(np.float32(np.log(np.float32(16.0))))
LN1_EPS = 1e-5
LN2_EPS = 1e-6


def build_program(S, ncores, last_phase=4, p4_mc=None, p4_cut=None, audit_sink=None, p4_dbg=False, vraw_waits=None):
    NCH = S // CH
    MC = 256
    NMC = S // MC
    TT = 512
    NTT = S // TT
    TCG = 256
    NTCG = S // TCG

    nc = bass.Bass("TRN2", target_bir_lowering=False, debug=False,
                   detect_race_conditions=False)

    d_x = nc.dram_tensor("x", [S, E], F32, kind="ExternalInput")
    d_qw = nc.dram_tensor("q_w", [E, E], F32, kind="ExternalInput")
    d_kw = nc.dram_tensor("k_w", [E, E], F32, kind="ExternalInput")
    d_vw = nc.dram_tensor("v_w", [E, E], F32, kind="ExternalInput")
    d_ow = nc.dram_tensor("out_w", [E, E], F32, kind="ExternalInput")
    d_qb = nc.dram_tensor("q_b", [1, E], F32, kind="ExternalInput")
    d_kb = nc.dram_tensor("k_b", [1, E], F32, kind="ExternalInput")
    d_vb = nc.dram_tensor("v_b", [1, E], F32, kind="ExternalInput")
    d_ob = nc.dram_tensor("out_b", [1, E], F32, kind="ExternalInput")
    d_projT = nc.dram_tensor("projT", [DH, M], F32, kind="ExternalInput")
    d_gamma = nc.dram_tensor("gamma", [1, E], F32, kind="ExternalInput")
    d_beta = nc.dram_tensor("beta", [1, E], F32, kind="ExternalInput")
    d_win = nc.dram_tensor("W_in", [E, FF], F32, kind="ExternalInput")
    d_bin = nc.dram_tensor("b_in", [1, FF], F32, kind="ExternalInput")
    d_wout = nc.dram_tensor("W_out", [FF, E], F32, kind="ExternalInput")
    d_bout = nc.dram_tensor("b_out", [1, E], F32, kind="ExternalInput")
    d_idn16 = nc.dram_tensor("idn16", [128, 128], F16, kind="ExternalInput")
    d_idnf = nc.dram_tensor("idnf", [128, 128], F32, kind="ExternalInput")
    d_maskT = nc.dram_tensor("maskT", [128, 128], F16, kind="ExternalInput")
    d_ones16 = nc.dram_tensor("ones16", [128, 128], F16, kind="ExternalInput")
    d_onesf = nc.dram_tensor("onesf", [128, 128], F32, kind="ExternalInput")
    d_c2d = nc.dram_tensor("c2d", [128, 1], F16, kind="ExternalInput")
    d_three = nc.dram_tensor("three", [128, 1], F32, kind="ExternalInput")
    d_mln16 = nc.dram_tensor("mln16", [128, 1], F32, kind="ExternalInput")
    d_out = nc.dram_tensor("out", [S, E], F32, kind="ExternalOutput")

    dbg_kind = "ExternalOutput" if p4_dbg else "Internal"
    d_hid = nc.dram_tensor("hid_scr", [FF, S], F16, kind=dbg_kind) if p4_dbg else nc.dram_tensor("hid_scr", [FF, S], F16)
    d_g = nc.dram_tensor("g_scr", [S, E], F16, kind=dbg_kind) if p4_dbg else nc.dram_tensor("g_scr", [S, E], F16)
    d_dbg = nc.dram_tensor("dbg", [128, 64], F32, kind="ExternalOutput") if p4_dbg else None
    d_cc1i = nc.dram_tensor("cc1i", [1, 128], F32)
    d_cc1o = nc.dram_tensor("cc1o", [1, 128], F32)
    d_cc2i = nc.dram_tensor("cc2i", [1, 128], F32)
    d_cc2o = nc.dram_tensor("cc2o", [1, 128], F32)

    es = ExitStack()

    def sb(name, shape, dt):
        return es.enter_context(nc.sbuf_tensor(name, shape, dt))

    class View:
        def __init__(self, pool, base, ncols):
            self.pool, self.base, self.ncols = pool, base, ncols

        def __getitem__(self, idx):
            rows, cols = idx
            if isinstance(cols, slice):
                start = 0 if cols.start is None else cols.start
                stop = self.ncols if cols.stop is None else cols.stop
                return self.pool[rows, self.base + start:self.base + stop]
            raise TypeError(cols)

    def psum(name, shape, dt):
        return es.enter_context(nc.psum_tensor(name, shape, dt))

    xnT = sb("xnT", [128, 8 * S], F16)
    wbuf = sb("wbuf", [128, 32768], F16)
    KV32 = sb("KV32", [128, 2048], F32)
    KV16 = sb("KV16", [128, 2048], F16)
    ks32 = sb("ks32", [128, 32], F32)
    ks16 = sb("ks16", [128, 32], F16)

    idn16 = sb("idn16s", [128, 128], F16)
    idnf = sb("idnfs", [128, 128], F32)
    maskT = sb("maskTs", [128, 128], F16)
    ones16 = sb("ones16s", [128, 128], F16)
    onesf = sb("onesfs", [128, 128], F32)
    c2d = sb("c2ds", [128, 1], F16)
    cmag = sb("cmag", [128, 1], F32)
    mln16 = sb("mln16s", [128, 1], F32)
    projs = sb("projs", [128, 256], F16)
    qbT = sb("qbT", [128, 8], F32)
    kbT = sb("kbT", [128, 8], F32)
    binT = sb("binT", [128, 32], F32)
    rows16 = sb("rows16", [128, E], F16)

    amaxr = sb("amaxr", [128, 1], F32)
    hmaxr = sb("hmaxr", [128, 1], F32)
    sc = sb("scl", [128, 12], F32)
    amaxf = sb("amaxf", [128, 32], F32)
    sw1 = sb("sw1", [128, 32], F32)
    invwx = sb("invwx", [128, 32], F32)

    S16 = sb("S16pool", [128, 16128], F16)
    S32 = sb("S32pool", [128, 4096], F32)
    xb = [sb("xb%d" % i, [128, E], F32) for i in range(2)]
    x2t = xb
    st1 = sb("st1", [128, 8], F32)
    mub = [sb("mub%d" % i, [128, 1], F32) for i in range(2)]
    rb = [sb("rb%d" % i, [128, 1], F32) for i in range(2)]
    xntok = [View(S16, 0, E), View(S16, E, E)]
    sqb = View(S16, 2048, E)
    winb = [View(S16, 0, FF), View(S16, FF, FF)]
    wmax = View(S16, 2 * FF, FF)
    scb = View(S16, 2 * FF, FF)
    xsim = [View(S16, 0, 8 * TT), View(S16, 4096, 8 * TT)]
    sty = View(S32, 0, FF)
    stq = View(S32, 0, TT)
    hstg = [View(S16, 12288, TT), View(S16, 12800, TT)]
    hred = sb("hred", [128, 2], F32)
    wob = [View(S32, 1024, E), View(S32, 2048, E)]
    wos = sb("wos", [128, 8], F32)
    sty2 = View(S32, 0, E)
    bor32v = View(S32, 3072, E)
    gstg = [View(S16, 0, E), View(S16, E, E)]
    hrl = [View(S16, 2048, TCG), View(S16, 2048 + TCG, TCG)]
    ihq = [View(S16, 2048 + 2 * TCG, TCG), View(S16, 2048 + 3 * TCG, TCG)]
    stq3 = View(S32, 0, TCG)
    QT = View(S16, 0, 8 * MC)
    KT = View(S16, 8 * MC, 8 * MC)
    ksq = View(S16, 16 * MC, 8 * MC)
    Vsb = View(S16, 24 * MC, 2 * E)
    atok = View(S16, 8192, E)
    atT = View(S16, 9216, E)
    sq2 = View(S16, 8192, E)
    g2t = [View(S16, 10240, E), View(S16, 11264, E)]
    QPb = [View(S16, 12288, 256), View(S16, 12544, 256)]
    KPb = [View(S16, 12800, 256), View(S16, 13056, 256)]
    kptb = [View(S16, 13312, 256), View(S16, 13568, 256)]
    Am16 = [View(S16, 13824, 128), View(S16, 13952, 128)]
    kpbias = sb("kpbias", [128, 16], F32)
    eckb = sb("eckb", [128, 16], F32)
    recipb = sb("recipb", [128, 16], F32)
    ab = [View(S32, 0, E), View(S32, 1024, E)]
    ab2 = ab
    ln2s = sb("ln2s", [128, 8], F32)
    gammab = View(S16, 14080, E)
    betab = View(S16, 15104, E)
    ccst = View(S32, 3072, 128)
    rhsD = View(S32, 1024, 128)
    ccst2 = View(S32, 3200, 128)
    projf = View(S32, 3328, 256)

    PSA = psum("PSA", [128, 1024], F32)
    PSB = psum("PSB", [128, 1024], F32)
    PSC = psum("PSC", [128, 512], F32)
    PSD = psum("PSD", [128, 512], F32)
    PSE = psum("PSE", [128, 512], F32)
    PST = psum("PST", [128, 1024], F16)

    sem_names = ["xl", "wl", "gw", "hs", "hr", "gs", "ccd", "cc", "xr", "gl",
                 "out", "T", "V", "SC", "G"]
    SEM = {n: es.enter_context(nc.semaphore("s_" + n)) for n in sem_names}

    steps = []
    cnt = defaultdict(int)
    vord = [0]

    def on(eng, fn):
        steps.append((eng, fn))

    def W(eng, s, v=None):
        t = cnt[s] if v is None else v
        assert t <= cnt[s], (s, t, cnt[s])
        if t > 0:
            on(eng, lambda e, _t=t, _s=s: e.wait_ge(SEM[_s], _t))

    def mark(s, by=1):
        cnt[s] += by
        return cnt[s]

    def E_(eng, s, by, fn):
        if eng == 'vector':
            o = vord[0]
            vord[0] += 1
            wv = vraw_waits.get(o) if vraw_waits is not None else None
            if wv:
                on('vector', lambda e, _t=wv: e.wait_ge(SEM['V'], _t))
            if s == 'V' and by == 1:
                mark('V', 1)
                on(eng, lambda e, _fn=fn: _fn(e).then_inc(SEM['V'], 1))
            else:
                raise AssertionError("vector ops must inc V")
            return
        mark(s, by)
        on(eng, lambda e, _fn=fn, _s=s, _by=by: _fn(e).then_inc(SEM[_s], _by))

    def Eq(eng, fn):
        if eng == 'vector':
            E_('vector', 'V', 1, fn)
            return
        on(eng, lambda e, _fn=fn: _fn(e))

    for (dst, src) in [(idn16, d_idn16), (idnf, d_idnf), (maskT, d_maskT),
                       (ones16, d_ones16), (onesf, d_onesf), (c2d, d_c2d),
                       (cmag, d_three), (mln16, d_mln16)]:
        E_('sync', 'wl', 16, lambda e, _d=dst, _s=src: e.dma_start(out=_d[:, :], in_=_s[:, :]))
    E_('sync', 'wl', 16, lambda e: e.dma_start(out=projf[0:64, :], in_=d_projT[:, :]))
    E_('sync', 'wl', 16, lambda e: e.dma_start(out=projf[64:128, :], in_=d_projT[:, :]))
    E_('sync', 'wl', 16, lambda e: e.dma_start(
        out=qbT[:, :], in_=bass.AP(d_qb, 0, [[1, 128], [128, 8]]),
        allow_slow_non_contiguous=True))
    E_('sync', 'wl', 16, lambda e: e.dma_start(
        out=kbT[:, :], in_=bass.AP(d_kb, 0, [[1, 128], [128, 8]]),
        allow_slow_non_contiguous=True))
    E_('sync', 'wl', 16, lambda e: e.dma_start(
        out=binT[:, :], in_=bass.AP(d_bin, 0, [[1, 128], [128, 32]]),
        allow_slow_non_contiguous=True))
    E_('gpsimd', 'gw', 16, lambda e: e.dma_start(out=rows16[0:1, :], in_=d_gamma[:, :]))
    E_('gpsimd', 'gw', 16, lambda e: e.dma_start(out=rows16[32:33, :], in_=d_beta[:, :]))

    wl_p0 = cnt['wl']
    gw_p0 = cnt['gw']

    W('vector', 'wl', wl_p0)
    Eq('vector', lambda e: e.reciprocal(cmag[:, :], cmag[:, :]))
    Eq('vector', lambda e: e.tensor_scalar(cmag[:, :], cmag[:, :], 4.0, 1.0, OP.mult, OP.subtract))
    Eq('vector', lambda e: e.tensor_scalar(cmag[:, :], cmag[:, :], 3.0, 1.0, OP.mult, OP.subtract))
    Eq('vector', lambda e: e.reciprocal(cmag[:, :], cmag[:, :]))
    E_('vector', 'V', 1, lambda e: e.tensor_single_scalar(cmag[:, :], cmag[:, :], 1.5, OP.mult))
    v_cmag = cnt['V']

    W('scalar', 'wl', wl_p0)
    E_('scalar', 'SC', 1, lambda e: e.activation(projs[:, :], projf[:, :], AF.Copy,
                                                 scale=1.0 / math.sqrt(DH)))
    W('tensor', 'wl', wl_p0)
    W('tensor', 'gw', gw_p0)
    E_('tensor', 'T', 1, lambda e: e.matmul(PSC[:, 0:512], ones16[0:1, 0:128], rows16[0:1, 0:512], start=True, stop=True))
    E_('tensor', 'T', 1, lambda e: e.matmul(PSD[:, 0:512], ones16[0:1, 0:128], rows16[0:1, 512:1024], start=True, stop=True))
    t_g1 = cnt['T']
    W('scalar', 'T', t_g1)
    E_('scalar', 'SC', 1, lambda e: e.activation(gammab[:, 0:512], PSC[:, 0:512], AF.Copy))
    E_('scalar', 'SC', 1, lambda e: e.activation(gammab[:, 512:1024], PSD[:, 0:512], AF.Copy))
    sc_g1 = cnt['SC']
    W('tensor', 'SC', sc_g1)
    E_('tensor', 'T', 1, lambda e: e.matmul(PSC[:, 0:512], ones16[32:33, 0:128], rows16[32:33, 0:512], start=True, stop=True))
    E_('tensor', 'T', 1, lambda e: e.matmul(PSD[:, 0:512], ones16[32:33, 0:128], rows16[32:33, 512:1024], start=True, stop=True))
    t_g2 = cnt['T']
    W('scalar', 'T', t_g2)
    E_('scalar', 'SC', 1, lambda e: e.activation(betab[:, 0:512], PSC[:, 0:512], AF.Copy))
    E_('scalar', 'SC', 1, lambda e: e.activation(betab[:, 512:1024], PSD[:, 0:512], AF.Copy))
    sc_g2 = cnt['SC']
    W('gpsimd', 'T', t_g2)
    E_('gpsimd', 'gw', 16, lambda e: e.dma_start(out=rows16[0:1, :], in_=d_vb[:, :]))
    E_('gpsimd', 'gw', 16, lambda e: e.dma_start(out=rows16[32:33, :], in_=d_ob[:, :]))

    v_xn = {}
    sc_ev = {}
    for ci in range(NCH):
        if ci >= 2:
            W('sync', 'V', v_xn[ci - 2])
        E_('sync', 'xl', 16, lambda e, _ci=ci: e.dma_start(
            out=xb[_ci % 2][:, :], in_=d_x[_ci * CH:(_ci + 1) * CH, :]))
        W('scalar', 'xl', 16 * (ci + 1))
        if ci >= 1:
            W('scalar', 'V', v_xn[ci - 1])
        E_('scalar', 'SC', 1, lambda e, _ci=ci: e.activation(sqb[:, :], xb[_ci % 2][:, :], AF.Square))
        sc_sq = cnt['SC']
        W('vector', 'xl', 16 * (ci + 1))
        Eq('vector', lambda e, _ci=ci: e.reduce_sum(st1[:, 0:1], xb[_ci % 2][:, :], axis=mybir.AxisListType.X))
        W('vector', 'SC', sc_sq)
        Eq('vector', lambda e: e.reduce_sum(st1[:, 1:2], sqb[:, :], axis=mybir.AxisListType.X))
        Eq('vector', lambda e, _ci=ci: e.tensor_single_scalar(mub[_ci % 2][:, :], st1[:, 0:1], 1.0 / E, OP.mult))
        Eq('vector', lambda e: e.tensor_single_scalar(st1[:, 2:3], st1[:, 1:2], 1.0 / E, OP.mult))
        Eq('vector', lambda e, _ci=ci: e.tensor_single_scalar(st1[:, 3:4], mub[_ci % 2][:, :], mub[_ci % 2][:, 0:1], OP.mult))
        Eq('vector', lambda e: e.tensor_sub(st1[:, 4:5], st1[:, 2:3], st1[:, 3:4]))
        Eq('vector', lambda e: e.tensor_single_scalar(st1[:, 4:5], st1[:, 4:5], LN1_EPS, OP.add))
        E_('vector', 'V', 1, lambda e: e.reciprocal(st1[:, 5:6], st1[:, 4:5]))
        v_vr = cnt['V']
        W('scalar', 'V', v_vr)
        E_('scalar', 'SC', 1, lambda e, _ci=ci: e.activation(rb[_ci % 2][:, :], st1[:, 5:6], AF.Sqrt))
        sc_r = cnt['SC']
        W('vector', 'SC', sc_r)
        Eq('vector', lambda e, _ci=ci: e.tensor_scalar(
            xntok[_ci % 2][:, :], xb[_ci % 2][:, :], mub[_ci % 2][:, 0:1], rb[_ci % 2][:, 0:1],
            OP.subtract, OP.mult))
        Eq('vector', lambda e, _ci=ci: e.reduce_max(st1[:, 6:7], xntok[_ci % 2][:, :],
                                                    axis=mybir.AxisListType.X, apply_absolute_value=True))
        if ci == 0:
            E_('vector', 'V', 1, lambda e: e.tensor_copy(amaxr[:, :], st1[:, 6:7]))
        else:
            E_('vector', 'V', 1, lambda e: e.tensor_max(amaxr[:, :], amaxr[:, :], st1[:, 6:7]))
        v_xn[ci] = cnt['V']
        W('tensor', 'V', v_xn[ci])
        if ci >= 1:
            W('tensor', 'SC', sc_ev[ci - 1])
        for b in range(8):
            if b < 7:
                Eq('tensor', lambda e, _ci=ci, _b=b: e.matmul(
                    PST[:, _b * 128:(_b + 1) * 128], xntok[_ci % 2][:, _b * 128:(_b + 1) * 128],
                    idn16[:, :], is_transpose=True, start=True, stop=True))
            else:
                E_('tensor', 'T', 1, lambda e, _ci=ci, _b=b: e.matmul(
                    PST[:, _b * 128:(_b + 1) * 128], xntok[_ci % 2][:, _b * 128:(_b + 1) * 128],
                    idn16[:, :], is_transpose=True, start=True, stop=True))
        t_tr = cnt['T']
        W('scalar', 'T', t_tr)
        for b in range(8):
            if b < 7:
                Eq('scalar', lambda e, _ci=ci, _b=b: e.activation(
                    xnT[:, _b * S + _ci * CH: _b * S + (_ci + 1) * CH],
                    PST[:, _b * 128:(_b + 1) * 128], AF.Copy))
            else:
                E_('scalar', 'SC', 1, lambda e, _ci=ci, _b=b: e.activation(
                    xnT[:, _b * S + _ci * CH: _b * S + (_ci + 1) * CH],
                    PST[:, _b * 128:(_b + 1) * 128], AF.Copy))
        sc_ev[ci] = cnt['SC']

    def amax_collective(src_run, cc_in, cc_out, col_amax, col_s, col_inv, qscale):
        W('tensor', 'V')
        E_('tensor', 'T', 1, lambda e: e.matmul(PSD[0:1, 0:128], src_run[:, 0:1], idnf[:, :],
                                                is_transpose=True, start=True, stop=True))
        W('vector', 'T')
        E_('vector', 'V', 1, lambda e: e.tensor_copy(ccst[0:1, 0:128], PSD[0:1, 0:128]))
        W('gpsimd', 'V')
        E_('gpsimd', 'ccd', 16, lambda e: e.dma_start(out=cc_in[0:1, 0:128], in_=ccst[0:1, 0:128]))
        W('gpsimd', 'ccd')
        E_('gpsimd', 'cc', 1, lambda e: e.collective_compute(
            "AllReduce", OP.max, replica_groups=[list(range(ncores))],
            ins=[cc_in.ap().opt()], outs=[cc_out.ap().opt()]))
        W('gpsimd', 'cc')
        E_('gpsimd', 'ccd', 16, lambda e: e.dma_start(out=ccst2[0:1, 0:128], in_=cc_out[0:1, 0:128]))
        W('vector', 'ccd')
        W('vector', 'cc')
        Eq('vector', lambda e: e.reduce_max(ccst[0:1, 0:1], ccst2[0:1, 0:128], axis=mybir.AxisListType.X))
        E_('vector', 'V', 1, lambda e: e.tensor_single_scalar(ccst[0:1, 0:1], ccst[0:1, 0:1], 1e-8, OP.max))
        W('tensor', 'V')
        E_('tensor', 'T', 1, lambda e: e.matmul(PSD[0:128, 128:129], onesf[0:1, 0:128],
                                                ccst[0:1, 0:1], start=True, stop=True))
        W('vector', 'T')
        Eq('vector', lambda e: e.tensor_copy(sc[:, col_amax:col_amax + 1], PSD[:, 128:129]))
        Eq('vector', lambda e: e.reciprocal(sc[:, 6:7], sc[:, col_amax:col_amax + 1]))
        Eq('vector', lambda e, _q=qscale: e.tensor_single_scalar(sc[:, col_s:col_s + 1], sc[:, 6:7], _q, OP.mult))
        E_('vector', 'V', 1, lambda e: e.reciprocal(sc[:, col_inv:col_inv + 1], sc[:, col_s:col_s + 1]))

    amax_collective(amaxr, d_cc1i, d_cc1o, 0, 1, 2, 32767.0)
    v_scales_x = cnt['V']

    for eb in range(8):
        if eb == 0:
            W('gpsimd', 'V', cnt['V'])
        else:
            W('gpsimd', 'SC')
        E_('gpsimd', 'gw', 16, lambda e, _eb=eb: e.dma_start(
            out=winb[0][:, :], in_=d_win[_eb * 128:(_eb + 1) * 128, :]))
        W('scalar', 'gw', cnt['gw'])
        if eb >= 2:
            W('scalar', 'V')
        E_('scalar', 'SC', 1, lambda e: e.activation(winb[1][:, :], winb[0][:, :], AF.Abs))
        W('vector', 'SC', cnt['SC'])
        if eb == 0:
            E_('vector', 'V', 1, lambda e: e.tensor_copy(wmax[:, :], winb[1][:, :]))
        else:
            E_('vector', 'V', 1, lambda e: e.tensor_tensor(
                wmax[:, :], wmax[:, :], winb[1][:, :], OP.max))
    v_wmax = cnt['V']
    for rnd in range(4):
        W('tensor', 'V', v_wmax)
        if rnd == 0:
            W('tensor', 'SC', sc_ev[NCH - 1])
        if rnd >= 1:
            W('tensor', 'V')
        for j in range(8):
            fb = rnd * 8 + j
            fn = lambda e, _fb=fb, _j=j: e.matmul(
                PST[:, _j * 128:(_j + 1) * 128], wmax[:, _fb * 128:(_fb + 1) * 128],
                idn16[:, :], is_transpose=True, start=True, stop=True)
            if j < 7:
                Eq('tensor', fn)
            else:
                E_('tensor', 'T', 1, fn)
        W('vector', 'T', cnt['T'])
        for j in range(8):
            fb = rnd * 8 + j
            fn = lambda e, _fb=fb, _j=j: e.reduce_max(
                amaxf[:, _fb:_fb + 1], PST[:, _j * 128:(_j + 1) * 128], axis=mybir.AxisListType.X)
            if j < 7:
                Eq('vector', fn)
            else:
                E_('vector', 'V', 1, fn)
    W('vector', 'V')
    Eq('vector', lambda e: e.tensor_single_scalar(amaxf[:, :], amaxf[:, :], 1e-8, OP.max))
    Eq('vector', lambda e: e.reciprocal(sw1[:, :], amaxf[:, :]))
    Eq('vector', lambda e: e.tensor_single_scalar(sw1[:, :], sw1[:, :], 127.0, OP.mult))
    Eq('vector', lambda e: e.reciprocal(invwx[:, :], sw1[:, :]))
    W('vector', 'V', v_scales_x)
    E_('vector', 'V', 1, lambda e: e.tensor_single_scalar(invwx[:, :], invwx[:, :], sc[:, 2:3], OP.mult))
    v_sw1 = cnt['V']
    for grp in range(8):
        for j in range(4):
            b = grp * 4 + j
            fn = lambda e, _b=b: e.tensor_scalar(
                rhsD[:, 0:128], idnf[:, :], sw1[:, _b:_b + 1], None, OP.mult)
            if b == 0:
                W('vector', 'V', v_sw1)
            else:
                W('vector', 'T')
            E_('vector', 'V', 1, fn)
            W('tensor', 'V', cnt['V'])
            if grp >= 1 and j == 0:
                W('tensor', 'SC')
            fn2 = lambda e, _j=j: e.matmul(
                PSD[:, _j * 128:(_j + 1) * 128], onesf[:, 0:128], rhsD[:, 0:128],
                start=True, stop=True)
            E_('tensor', 'T', 1, fn2)
        W('scalar', 'T', cnt['T'])
        fn = lambda e, _g=grp: e.activation(scb[:, _g * 512:(_g + 1) * 512], PSD[:, 0:512], AF.Copy)
        E_('scalar', 'SC', 1, fn)
    sc_scb = cnt['SC']
    W('vector', 'SC', sc_scb)
    W('vector', 'V', v_cmag)
    for eb in range(8):
        if eb >= 2:
            W('gpsimd', 'V', cnt['V'])
        else:
            W('gpsimd', 'V', v_wmax)
        E_('gpsimd', 'gw', 16, lambda e, _eb=eb: e.dma_start(
            out=winb[_eb % 2][:, :], in_=d_win[_eb * 128:(_eb + 1) * 128, :]))
        W('vector', 'gw', cnt['gw'])
        Eq('vector', lambda e, _eb=eb: e.tensor_tensor(sty[:, :], winb[_eb % 2][:, :], scb[:, :], OP.mult))
        E_('vector', 'V', 1, lambda e, _eb=eb: e.tensor_scalar(
            wbuf[:, _eb * FF:(_eb + 1) * FF], sty[:, :], cmag[:, 0:1], cmag[:, 0:1], OP.add, OP.subtract))
    v_win_ready = cnt['V']

    sc_gelu = {}
    g_hdone = {}
    t_ffn1 = {}
    for tt in range(NTT):
        W('vector', 'V', v_scales_x)
        if tt >= 2:
            W('tensor', 'V')
        if tt >= 2:
            W('vector', 'T', t_ffn1[tt - 2])
        for eb in range(8):
            Eq('vector', lambda e, _tt=tt, _eb=eb: e.tensor_scalar(
                stq[:, :], xnT[:, _eb * S + _tt * TT: _eb * S + (_tt + 1) * TT],
                sc[:, 1:2], cmag[:, 0:1], OP.mult, OP.add))
            fn = lambda e, _tt=tt, _eb=eb: e.tensor_scalar(
                xsim[_tt % 2][:, _eb * TT:(_eb + 1) * TT], stq[:, :], cmag[:, 0:1], None, OP.subtract)
            if eb < 7:
                Eq('vector', fn)
            else:
                E_('vector', 'V', 1, fn)
        v_xsim = cnt['V']
        W('tensor', 'V', v_xsim)
        if tt == 0:
            W('tensor', 'V', v_win_ready)
        for fb in range(32):
            ps = PSC if fb % 2 == 0 else PSD
            if fb >= 2:
                W('tensor', 'SC', sc_gelu[(tt, fb - 2)])
            elif tt >= 1:
                W('tensor', 'SC', sc_gelu[(tt - 1, fb + 30)])
            for kb in range(8):
                fn = lambda e, _ps=ps, _fb=fb, _kb=kb, _tt=tt: e.matmul(
                    _ps[:, 0:TT], wbuf[:, _kb * FF + _fb * 128: _kb * FF + (_fb + 1) * 128],
                    xsim[_tt % 2][:, _kb * TT:(_kb + 1) * TT],
                    start=(_kb == 0), stop=(_kb == 7))
                if kb < 7:
                    Eq('tensor', fn)
                else:
                    E_('tensor', 'T', 1, fn)
            t_mm = cnt['T']
            W('scalar', 'T', t_mm)
            if (tt * 32 + fb) >= 2:
                pi = (tt * 32 + fb) - 2
                W('scalar', 'V', g_hdone.get(pi, 0))
                W('scalar', 'hs', 16 * (pi + 1))
            E_('scalar', 'SC', 1, lambda e, _ps=ps, _fb=fb: e.activation(
                hstg[_fb % 2][:, :], _ps[:, 0:TT], AF.Gelu,
                bias=binT[:, _fb:_fb + 1], scale=invwx[:, _fb:_fb + 1]))
            sc_gelu[(tt, fb)] = cnt['SC']
            W('vector', 'SC', cnt['SC'])
            Eq('vector', lambda e, _fb=fb: e.reduce_max(
                hred[:, 0:1], hstg[_fb % 2][:, :], axis=mybir.AxisListType.X,
                apply_absolute_value=True))
            if tt == 0 and fb == 0:
                E_('vector', 'V', 1, lambda e: e.tensor_copy(hmaxr[:, :], hred[:, 0:1]))
            else:
                E_('vector', 'V', 1, lambda e: e.tensor_max(hmaxr[:, :], hmaxr[:, :], hred[:, 0:1]))
            g_hdone[tt * 32 + fb] = cnt['V']
            W('gpsimd', 'SC', cnt['SC'])
            E_('gpsimd', 'hs', 16, lambda e, _tt=tt, _fb=fb: e.dma_start(
                out=d_hid[_fb * 128:(_fb + 1) * 128, _tt * TT:(_tt + 1) * TT],
                in_=hstg[_fb % 2][:, :]))
        t_ffn1[tt] = cnt['T']

    if last_phase < 3:
        W('sync', 'hs', cnt['hs'])
        if audit_sink is not None:
            audit_sink['steps'] = steps; audit_sink['SEM'] = SEM; es.close(); return nc
        with es, nc.Block() as block:
            for eng in ("sync", "gpsimd", "vector", "scalar", "tensor"):
                def body(e, _eng=eng):
                    for (tag, fn) in steps:
                        if tag == _eng:
                            fn(e)
                getattr(block, eng)(body)
        return nc
    amax_collective(hmaxr, d_cc2i, d_cc2o, 3, 4, 5, 32767.0)
    v_scales_h = cnt['V']
    W('sync', 'V', cnt['V'])
    E_('sync', 'xr', 16, lambda e: e.dma_start(out=bor32v[0:1, :], in_=d_bout[:, :]))
    W('vector', 'xr', cnt['xr'])
    E_('vector', 'V', 1, lambda e: e.tensor_single_scalar(rows16[64:65, :], bor32v[0:1, :], sc[64:65, 4:5], OP.mult))
    v_bor = cnt['V']

    sc_wo_start = cnt['SC']
    W('sync', 'V', cnt['V'])
    for fb in range(32):
        if fb >= 2:
            W('sync', 'V')
        E_('sync', 'xr', 16, lambda e, _fb=fb: e.dma_start(
            out=wob[_fb % 2][:, :], in_=d_wout[_fb * 128:(_fb + 1) * 128, :]))
        W('vector', 'xr', cnt['xr'])
        W('vector', 'T', t_ffn1[NTT - 1])
        Eq('vector', lambda e, _fb=fb: e.reduce_max(
            wos[:, 0:1], wob[_fb % 2][:, :], axis=mybir.AxisListType.X, apply_absolute_value=True))
        Eq('vector', lambda e: e.tensor_single_scalar(wos[:, 0:1], wos[:, 0:1], 1e-8, OP.max))
        Eq('vector', lambda e: e.reciprocal(wos[:, 1:2], wos[:, 0:1]))
        Eq('vector', lambda e: e.tensor_single_scalar(wos[:, 1:2], wos[:, 1:2], 127.0, OP.mult))
        Eq('vector', lambda e: e.reciprocal(wos[:, 2:3], wos[:, 1:2]))
        Eq('vector', lambda e, _fb=fb: e.tensor_scalar(
            sty2[:, :], wob[_fb % 2][:, :], wos[:, 1:2], cmag[:, 0:1], OP.mult, OP.add))
        E_('vector', 'V', 1, lambda e, _fb=fb: e.tensor_scalar(
            wbuf[:, _fb * E:(_fb + 1) * E], sty2[:, :], cmag[:, 0:1], wos[:, 2:3],
            OP.subtract, OP.mult))
    v_wout_ready = cnt['V']

    sc_silu = {}
    t_ffn2 = {}
    g_gst = {}
    W('gpsimd', 'T', t_ffn1[NTT - 1])
    W('gpsimd', 'hs', cnt['hs'])
    for tg in range(NTCG):
        for fb in range(32):
            idx = tg * 32 + fb
            if idx >= 2:
                W('gpsimd', 'V')
            E_('gpsimd', 'hr', 16, lambda e, _tg=tg, _fb=fb: e.dma_start(
                out=hrl[_fb % 2][:, :], in_=d_hid[_fb * 128:(_fb + 1) * 128, _tg * TCG:(_tg + 1) * TCG]))
            W('vector', 'hr', cnt['hr'])
            W('vector', 'V', v_scales_h)
            if idx >= 2:
                W('vector', 'T', t_ffn2.get(idx - 2, 0))
            Eq('vector', lambda e, _fb=fb: e.tensor_scalar(
                stq3[:, :], hrl[_fb % 2][:, :], sc[:, 4:5], cmag[:, 0:1], OP.mult, OP.add))
            E_('vector', 'V', 1, lambda e, _fb=fb: e.tensor_scalar(
                ihq[_fb % 2][:, :], stq3[:, :], cmag[:, 0:1], None, OP.subtract))
            W('tensor', 'V', cnt['V'])
            if fb == 0:
                W('tensor', 'V', v_wout_ready)
                if tg >= 1:
                    W('tensor', 'SC', sc_silu[(tg - 1, 1)])
            for tc in range(2):
                PS = PSA if tc == 0 else PSB
                for hf in range(2):
                    fn = lambda e, _PS=PS, _fb=fb, _tc=tc, _hf=hf: e.matmul(
                        _PS[:, _hf * 512:(_hf + 1) * 512],
                        ihq[_fb % 2][:, _tc * 128:(_tc + 1) * 128],
                        wbuf[:, _fb * E + _hf * 512: _fb * E + (_hf + 1) * 512],
                        start=(_fb == 0), stop=False)
                    if tc == 1 and hf == 1:
                        E_('tensor', 'T', 1, fn)
                    else:
                        Eq('tensor', fn)
            t_ffn2[idx] = cnt['T']
        W('tensor', 'V', v_bor)
        for tc in range(2):
            PS = PSA if tc == 0 else PSB
            for hf in range(2):
                fn = lambda e, _PS=PS, _tc=tc, _hf=hf: e.matmul(
                    _PS[:, _hf * 512:(_hf + 1) * 512], ones16[64:65, 0:128],
                    rows16[64:65, _hf * 512:(_hf + 1) * 512], start=False, stop=True)
                if hf == 1:
                    E_('tensor', 'T', 1, fn)
                else:
                    Eq('tensor', fn)
            t_bias = cnt['T']
            W('scalar', 'T', t_bias)
            if tg >= 1:
                W('scalar', 'gs', 16 * (2 * (tg - 1) + tc + 1))
            E_('scalar', 'SC', 1, lambda e, _PS=PS, _tc=tc: e.activation(
                gstg[_tc][:, :], _PS[:, 0:1024], AF.Silu, scale=sc[:, 5:6]))
            sc_silu[(tg, tc)] = cnt['SC']
            W('gpsimd', 'SC', cnt['SC'])
            E_('gpsimd', 'gs', 16, lambda e, _tg=tg, _tc=tc: e.dma_start(
                out=d_g[_tg * TCG + _tc * 128: _tg * TCG + (_tc + 1) * 128, :],
                in_=gstg[_tc][:, :]))

    if last_phase < 4:
        W('sync', 'gs', cnt['gs'])
        if audit_sink is not None:
            audit_sink['steps'] = steps; audit_sink['SEM'] = SEM; es.close(); return nc
        with es, nc.Block() as block:
            for eng in ("sync", "gpsimd", "vector", "scalar", "tensor"):
                def body(e, _eng=eng):
                    for (tag, fn) in steps:
                        if tag == _eng:
                            fn(e)
                getattr(block, eng)(body)
        return nc
    t_last_ffn2 = cnt['T']
    W('gpsimd', 'T', t_last_ffn2)
    for wi, dw in enumerate([d_qw, d_kw, d_vw, d_ow]):
        for eb in range(8):
            fn = lambda e, _wi=wi, _eb=eb, _dw=dw: e.dma_start(
                out=wbuf[:, _wi * 8192 + _eb * 1024: _wi * 8192 + (_eb + 1) * 1024],
                in_=_dw[_eb * 128:(_eb + 1) * 128, :])
            E_('gpsimd', 'gl', 16, fn)
    gl_w = cnt['gl']
    W('sync', 'gs', cnt['gs'])
    W('scalar', 'gs', cnt['gs'])
    W('scalar', 'V', cnt['V'])
    QWO, KWO, VWO, OWO = 0, 8192, 16384, 24576

    E_('gpsimd', 'G', 1, lambda e: e.memset(KV32[:, :], 0.0))
    E_('gpsimd', 'G', 1, lambda e: e.memset(KV16[:, :], 0.0))
    E_('gpsimd', 'G', 1, lambda e: e.memset(ks32[:, :], 0.0))
    E_('gpsimd', 'G', 1, lambda e: e.memset(ks16[:, :], 0.0))
    g_state0 = cnt['G']

    if p4_mc is not None:
        _NMC_RUN = p4_mc
    else:
        _NMC_RUN = NMC
    v_state = 0
    g_state = g_state0
    sc_out_prev = {}
    v_ab_done = {}
    for mc in range(_NMC_RUN):
        W('tensor', 'gl', gl_w)
        W('tensor', 'SC', cnt['SC'])
        W('tensor', 'V', cnt['V'])
        for which, (WO, dst, bia) in enumerate([(QWO, QT, qbT), (KWO, KT, kbT)]):
            for b in range(8):
                PS = PSA if b < 4 else PSB
                off = (b % 4) * 256
                for kb in range(8):
                    fn = lambda e, _PS=PS, _off=off, _WO=WO, _b=b, _kb=kb, _mc=mc: e.matmul(
                        _PS[:, _off:_off + MC],
                        wbuf[:, _WO + _kb * 1024 + _b * 128: _WO + _kb * 1024 + (_b + 1) * 128],
                        xnT[:, _kb * S + _mc * MC: _kb * S + (_mc + 1) * MC],
                        start=(_kb == 0), stop=(_kb == 7))
                    if kb < 7:
                        Eq('tensor', fn)
                    else:
                        E_('tensor', 'T', 1, fn)
            t_qk = cnt['T']
            W('scalar', 'T', t_qk)
            for b in range(8):
                PS = PSA if b < 4 else PSB
                off = (b % 4) * 256
                fn = lambda e, _PS=PS, _off=off, _b=b, _dst=dst, _bia=bia: e.activation(
                    _dst[:, _b * MC:(_b + 1) * MC], _PS[:, _off:_off + MC], AF.Identity,
                    bias=_bia[:, _b:_b + 1])
                if b < 7:
                    Eq('scalar', fn)
                else:
                    E_('scalar', 'SC', 1, fn)
            sc_qk = cnt['SC']
            W('tensor', 'SC', sc_qk)
        for b in range(8):
            fn = lambda e, _b=b: e.activation(ksq[:, _b * MC:(_b + 1) * MC],
                                              KT[:, _b * MC:(_b + 1) * MC], AF.Square)
            if b < 7:
                Eq('scalar', fn)
            else:
                E_('scalar', 'SC', 1, fn)
        sc_ksq = cnt['SC']
        for tc in range(2):
            PS = PSA if tc == 0 else PSB
            for hf in range(2):
                for kb in range(8):
                    Eq('tensor', lambda e, _PS=PS, _hf=hf, _kb=kb, _mc=mc, _tc=tc: e.matmul(
                        _PS[:, _hf * 512:(_hf + 1) * 512],
                        xnT[:, _kb * S + (_mc * 2 + _tc) * 128: _kb * S + (_mc * 2 + _tc + 1) * 128],
                        wbuf[:, VWO + _kb * 1024 + _hf * 512: VWO + _kb * 1024 + (_hf + 1) * 512],
                        start=(_kb == 0), stop=False))
                fn = lambda e, _PS=PS, _hf=hf: e.matmul(
                    _PS[:, _hf * 512:(_hf + 1) * 512], ones16[0:1, 0:128],
                    rows16[0:1, _hf * 512:(_hf + 1) * 512], start=False, stop=True)
                if hf == 1 and tc == 1:
                    E_('tensor', 'T', 1, fn)
                else:
                    Eq('tensor', fn)
        t_v = cnt['T']
        W('scalar', 'T', t_v)
        for tc in range(2):
            PS = PSA if tc == 0 else PSB
            fn = lambda e, _PS=PS, _tc=tc: e.activation(Vsb[:, _tc * E:(_tc + 1) * E],
                                                        _PS[:, 0:1024], AF.Copy)
            if tc == 1:
                E_('scalar', 'SC', 1, fn)
            else:
                Eq('scalar', fn)
        sc_v = cnt['SC']
        if p4_cut == 'qkv':
            break

        for tci in range(2):
            ci = mc * 2 + tci
            tok0 = tci * 128
            W('tensor', 'SC', sc_v)
            W('tensor', 'SC', cnt['SC'])
            W('tensor', 'V', cnt['V'])
            for h in range(16):
                po = 64 * (h % 2)
                co = (h // 2) * MC + tok0
                fn = lambda e, _po=po, _co=co, _h=h: e.matmul(
                    PSD[:, 256 + _h:256 + _h + 1], ksq[_po:_po + 64, _co:_co + 128],
                    c2d[_po:_po + 64, 0:1], start=True, stop=True)
                if h < 15:
                    Eq('tensor', fn)
                else:
                    E_('tensor', 'T', 1, fn)
            t_ck = cnt['T']
            W('vector', 'T', t_ck)
            E_('vector', 'V', 1, lambda e: e.tensor_scalar(
                kpbias[:, :], PSD[:, 256:272], -1.0, -LN16, OP.mult, OP.add))
            v_kpb = cnt['V']
            W('scalar', 'T', t_ck)
            W('scalar', 'V', v_kpb)
            E_('scalar', 'SC', 1, lambda e: e.activation(eckb[:, :], PSD[:, 256:272], AF.Exp, scale=-1.0))
            sc_eck = cnt['SC']

            W('tensor', 'G', g_state)
            W('tensor', 'V', v_state)
            W('tensor', 'SC', sc_eck)
            W('tensor', 'V', v_kpb)
            for h in range(16):
                hb = h % 2
                po = 64 * (h % 2)
                co = (h // 2) * MC + tok0
                if h % 4 == 0 and h >= 4:
                    W('tensor', 'V')
                if h >= 1:
                    W('tensor', 'SC')
                    W('tensor', 'V')
                for mb in range(2):
                    Eq('tensor', lambda e, _mb=mb, _po=po, _co=co: e.matmul(
                        PSC[:, _mb * 128:(_mb + 1) * 128], projs[_po:_po + 64, _mb * 128:(_mb + 1) * 128],
                        QT[_po:_po + 64, _co:_co + 128], start=True, stop=True))
                for mb in range(2):
                    Eq('tensor', lambda e, _mb=mb, _po=po, _co=co: e.matmul(
                        PSC[:, 256 + _mb * 128:256 + (_mb + 1) * 128], projs[_po:_po + 64, _mb * 128:(_mb + 1) * 128],
                        KT[_po:_po + 64, _co:_co + 128], start=True, stop=True))
                E_('tensor', 'T', 1, lambda e, _po=po, _co=co: e.matmul(
                    PSD[:, 0:256], KT[_po:_po + 64, _co:_co + 128], projs[_po:_po + 64, 0:256],
                    start=True, stop=True))
                t_maps = cnt['T']
                W('scalar', 'T', t_maps)
                W('scalar', 'SC', sc_eck)
                Eq('scalar', lambda e, _hb=hb: e.activation(QPb[_hb][:, :], PSC[:, 0:256], AF.Exp))
                Eq('scalar', lambda e, _hb=hb: e.activation(KPb[_hb][:, :], PSC[:, 256:512], AF.Exp, bias=mln16[:, 0:1]))
                E_('scalar', 'SC', 1, lambda e, _hb=hb, _h=h: e.activation(
                    kptb[_hb][:, :], PSD[:, 0:256], AF.Exp, bias=kpbias[:, _h:_h + 1]))
                sc_exp = cnt['SC']
                W('tensor', 'SC', sc_exp)
                Eq('tensor', lambda e, _hb=hb: e.matmul(
                    PSD[:, 320:448], KPb[_hb][:, 0:128], QPb[_hb][:, 0:128], start=True, stop=False))
                E_('tensor', 'T', 1, lambda e, _hb=hb: e.matmul(
                    PSD[:, 320:448], KPb[_hb][:, 128:256], QPb[_hb][:, 128:256], start=False, stop=True))
                t_A = cnt['T']
                W('vector', 'T', t_A)
                W('vector', 'SC', sc_eck)
                E_('vector', 'V', 1, lambda e, _hb=hb, _h=h: e.scalar_tensor_tensor(
                    Am16[_hb][:, :], PSD[:, 320:448], eckb[:, _h:_h + 1], maskT[:, :],
                    OP.mult, OP.mult))
                v_am = cnt['V']
                W('tensor', 'V', v_am)
                Eq('tensor', lambda e, _hb=hb, _h=h, _tci=tci: e.matmul(
                    PSA[:, _h * 64:(_h + 1) * 64], Am16[_hb][:, :],
                    Vsb[:, _tci * E + _h * 64: _tci * E + (_h + 1) * 64], start=True, stop=False))
                for b in range(2):
                    Eq('tensor', lambda e, _hb=hb, _h=h, _b=b: e.matmul(
                        PSA[:, _h * 64:(_h + 1) * 64], QPb[_hb][:, _b * 128:(_b + 1) * 128],
                        KV16[:, (2 * _h + _b) * 64:(2 * _h + _b + 1) * 64],
                        start=False, stop=(_b == 1)))
                Eq('tensor', lambda e, _hb=hb, _h=h: e.matmul(
                    PSD[:, 272 + _h:272 + _h + 1], Am16[_hb][:, :], ones16[:, 0:1],
                    start=True, stop=False))
                for b in range(2):
                    Eq('tensor', lambda e, _hb=hb, _h=h, _b=b: e.matmul(
                        PSD[:, 272 + _h:272 + _h + 1], QPb[_hb][:, _b * 128:(_b + 1) * 128],
                        ks16[:, 2 * _h + _b:2 * _h + _b + 1], start=False, stop=(_b == 1)))
                for b in range(2):
                    Eq('tensor', lambda e, _hb=hb, _h=h, _b=b, _tci=tci: e.matmul(
                        PSE[:, (_h % 4) * 128 + _b * 64:(_h % 4) * 128 + (_b + 1) * 64],
                        kptb[_hb][:, _b * 128:(_b + 1) * 128],
                        Vsb[:, _tci * E + _h * 64: _tci * E + (_h + 1) * 64],
                        start=True, stop=True))
                for b in range(2):
                    fn = lambda e, _hb=hb, _h=h, _b=b: e.matmul(
                        PSD[:, 288 + (_h % 4) * 2 + _b:288 + (_h % 4) * 2 + _b + 1],
                        kptb[_hb][:, _b * 128:(_b + 1) * 128], ones16[:, 0:1],
                        start=True, stop=True)
                    if b < 1:
                        Eq('tensor', fn)
                    else:
                        E_('tensor', 'T', 1, fn)
                t_head = cnt['T']
                if h % 4 == 3:
                    g0 = h - 3
                    W('vector', 'T', t_head)
                    Eq('vector', lambda e, _g0=g0: e.tensor_tensor(
                        KV32[:, _g0 * 128:(_g0 + 4) * 128], KV32[:, _g0 * 128:(_g0 + 4) * 128],
                        PSE[:, 0:512], OP.add))
                    E_('vector', 'V', 1, lambda e, _g0=g0: e.tensor_tensor(
                        ks32[:, _g0 * 2:(_g0 + 4) * 2], ks32[:, _g0 * 2:(_g0 + 4) * 2],
                        PSD[:, 288:296], OP.add))
                    v_upd = cnt['V']
                    W('gpsimd', 'V', v_upd)
                    Eq('gpsimd', lambda e, _g0=g0: e.tensor_copy(
                        KV16[:, _g0 * 128:(_g0 + 4) * 128], KV32[:, _g0 * 128:(_g0 + 4) * 128]))
                    E_('gpsimd', 'G', 1, lambda e, _g0=g0: e.tensor_copy(
                        ks16[:, _g0 * 2:(_g0 + 4) * 2], ks32[:, _g0 * 2:(_g0 + 4) * 2]))
            g_state = cnt['G']
            v_state = cnt['V']
            if p4_cut == 'heads':
                break

            W('vector', 'T')
            E_('vector', 'V', 1, lambda e: e.reciprocal(recipb[:, :], PSD[:, 272:288]))
            for h in range(16):
                fn = lambda e, _h=h: e.tensor_scalar(
                    atok[:, _h * 64:(_h + 1) * 64], PSA[:, _h * 64:(_h + 1) * 64],
                    recipb[:, _h:_h + 1], None, OP.mult)
                if h < 15:
                    Eq('vector', fn)
                else:
                    E_('vector', 'V', 1, fn)
            v_div = cnt['V']
            if p4_cut == 'div':
                break
            W('tensor', 'V', v_div)
            if ci >= 1:
                W('tensor', 'SC', sc_out_prev.get(ci - 1, 0))
            for b in range(8):
                fn = lambda e, _b=b: e.matmul(
                    PST[:, _b * 128:(_b + 1) * 128], atok[:, _b * 128:(_b + 1) * 128],
                    idn16[:, :], is_transpose=True, start=True, stop=True)
                if b < 7:
                    Eq('tensor', fn)
                else:
                    E_('tensor', 'T', 1, fn)
            t_att = cnt['T']
            W('scalar', 'T', t_att)
            E_('scalar', 'SC', 1, lambda e: e.activation(atT[:, :], PST[:, 0:1024], AF.Copy))
            sc_att = cnt['SC']
            sc_out_prev[ci] = sc_att
            if p4_cut == 'tr':
                break
            W('tensor', 'SC', sc_att)
            if ci >= 1:
                W('tensor', 'V', v_ab_done.get(ci - 1, 0))
            for hf in range(2):
                for kb in range(8):
                    Eq('tensor', lambda e, _hf=hf, _kb=kb: e.matmul(
                        PSB[:, _hf * 512:(_hf + 1) * 512], atT[:, _kb * 128:(_kb + 1) * 128],
                        wbuf[:, OWO + _kb * 1024 + _hf * 512: OWO + _kb * 1024 + (_hf + 1) * 512],
                        start=(_kb == 0), stop=False))
                Eq('tensor', lambda e, _hf=hf: e.matmul(
                    PSB[:, _hf * 512:(_hf + 1) * 512], ones16[32:33, 0:128],
                    rows16[32:33, _hf * 512:(_hf + 1) * 512], start=False, stop=False))
                for j in range(4):
                    b = hf * 4 + j
                    fn = lambda e, _hf=hf, _j=j, _b=b, _ci=ci: e.matmul(
                        PSB[:, _hf * 512 + _j * 128:_hf * 512 + (_j + 1) * 128],
                        xnT[:, _b * S + _ci * CH: _b * S + (_ci + 1) * CH],
                        idn16[:, :], start=False, stop=(_j == 3))
                    if hf == 1 and j == 3:
                        E_('tensor', 'T', 1, fn)
                    else:
                        Eq('tensor', fn)
            t_proj = cnt['T']
            if p4_cut == 'proj':
                break
            W('scalar', 'T', t_proj)
            if ci >= 1:
                W('scalar', 'V', v_ab_done.get(ci - 1, 0))
            E_('scalar', 'SC', 1, lambda e: e.activation(sq2[:, :], PSB[:, 0:1024], AF.Square))
            sc_sq2 = cnt['SC']
            W('vector', 'T', t_proj)
            W('vector', 'SC', sc_sq2)
            Eq('vector', lambda e: e.reduce_sum(ln2s[:, 0:1], PSB[:, 0:512], axis=mybir.AxisListType.X))
            Eq('vector', lambda e: e.reduce_sum(ln2s[:, 1:2], PSB[:, 512:1024], axis=mybir.AxisListType.X))
            Eq('vector', lambda e: e.tensor_add(ln2s[:, 0:1], ln2s[:, 0:1], ln2s[:, 1:2]))
            W('vector', 'SC', sc_sq2)
            Eq('vector', lambda e: e.reduce_sum(ln2s[:, 2:3], sq2[:, :], axis=mybir.AxisListType.X))
            if p4_cut == 'sq':
                mark('V', 1)
                on('vector', lambda e: e.tensor_copy(ln2s[:, 0:1], ln2s[:, 0:1]).then_inc(SEM['V'], 1))
                break
            Eq('vector', lambda e: e.tensor_single_scalar(ln2s[:, 3:4], ln2s[:, 0:1], 1.0 / E, OP.mult))
            Eq('vector', lambda e: e.tensor_single_scalar(ln2s[:, 4:5], ln2s[:, 2:3], 1.0 / E, OP.mult))
            Eq('vector', lambda e: e.tensor_single_scalar(ln2s[:, 5:6], ln2s[:, 3:4], ln2s[:, 3:4], OP.mult))
            Eq('vector', lambda e: e.tensor_sub(ln2s[:, 4:5], ln2s[:, 4:5], ln2s[:, 5:6]))
            Eq('vector', lambda e: e.tensor_single_scalar(ln2s[:, 4:5], ln2s[:, 4:5], LN2_EPS, OP.add))
            E_('vector', 'V', 1, lambda e: e.reciprocal(ln2s[:, 6:7], ln2s[:, 4:5]))
            v_ln2 = cnt['V']
            W('scalar', 'V', v_ln2)
            E_('scalar', 'SC', 1, lambda e: e.activation(ln2s[:, 7:8], ln2s[:, 6:7], AF.Sqrt))
            sc_r2 = cnt['SC']
            if p4_cut == 'ln2':
                break
            if ci >= 2:
                W('sync', 'V', v_ab_done.get(ci - 2, 0))
            E_('sync', 'xr', 16, lambda e, _ci=ci: e.dma_start(
                out=x2t[_ci % 2][:, :], in_=d_x[_ci * CH:(_ci + 1) * CH, :]))
            E_('sync', 'xr', 16, lambda e, _ci=ci: e.dma_start(
                out=g2t[_ci % 2][:, :], in_=d_g[_ci * CH:(_ci + 1) * CH, :]))
            xr_t = cnt['xr']
            W('vector', 'SC', sc_r2)
            Eq('vector', lambda e, _ci=ci: e.tensor_scalar(
                ab[_ci % 2][:, :], PSB[:, 0:1024], ln2s[:, 3:4], ln2s[:, 7:8],
                OP.subtract, OP.mult))
            if p4_cut == 'a1':
                mark('V', 1)
                on('vector', lambda e: e.tensor_copy(ln2s[:, 0:1], ln2s[:, 0:1]).then_inc(SEM['V'], 1))
                break
            Eq('vector', lambda e, _ci=ci: e.tensor_tensor(ab[_ci % 2][:, :], ab[_ci % 2][:, :], gammab[:, :], OP.mult))
            Eq('vector', lambda e, _ci=ci: e.tensor_tensor(ab[_ci % 2][:, :], ab[_ci % 2][:, :], betab[:, :], OP.add))
            W('vector', 'xr', xr_t)
            Eq('vector', lambda e, _ci=ci: e.tensor_tensor(ab[_ci % 2][:, :], ab[_ci % 2][:, :], g2t[_ci % 2][:, :], OP.mult))
            if p4_cut == 'a4':
                mark('V', 1)
                on('vector', lambda e: e.tensor_copy(ln2s[:, 0:1], ln2s[:, 0:1]).then_inc(SEM['V'], 1))
                break
            W('vector', 'xr', xr_t)
            if ci >= 2:
                W('vector', 'out', 16 * (ci - 1))
            E_('vector', 'V', 1, lambda e, _ci=ci: e.tensor_tensor(
                ab2[_ci % 2][:, :], ab[_ci % 2][:, :], x2t[_ci % 2][:, :], OP.add))
            v_ab = cnt['V']
            v_ab_done[ci] = v_ab
            W('sync', 'V', v_ab)
            E_('sync', 'out', 16, lambda e, _ci=ci: e.dma_start(
                out=d_out[_ci * CH:(_ci + 1) * CH, :], in_=ab2[_ci % 2][:, :]))
        if p4_cut in ('heads', 'div', 'tr', 'proj', 'ln2', 'a1', 'a4', 'sq'):
            break

    if p4_dbg:
        W('vector', 'T', cnt['T'])
        W('vector', 'SC', cnt['SC'])
        Eq('vector', lambda e: e.tensor_copy(st1[:, 0:8], sc[:, 0:8]))
        Eq('vector', lambda e: e.tensor_copy(xb[0][:, 0:12], sc[:, 0:12]))
        E_('vector', 'V', 1, lambda e: e.tensor_copy(xb[0][:, 12:13], cmag[:, 0:1]))
        Eq('vector', lambda e: e.tensor_copy(xb[0][:, 16:32], kpbias[:, 0:16]))
        Eq('vector', lambda e: e.tensor_copy(xb[0][:, 32:48], eckb[:, 0:16]))
        E_('vector', 'V', 1, lambda e: e.tensor_copy(xb[0][:, 48:64], recipb[:, 0:16]))
        W('sync', 'V', cnt['V'])
        E_('sync', 'out', 16, lambda e: e.dma_start(out=d_dbg[:, :], in_=xb[0][:, 0:64]))
    if p4_cut is not None or (p4_mc is not None and p4_mc < NMC):
        W('sync', 'gl', cnt['gl'])
        W('sync', 'T', cnt['T'])
        W('sync', 'V', cnt['V'])
        W('sync', 'SC', cnt['SC'])
        W('sync', 'G', cnt['G'])
    W('sync', 'out', cnt['out'])

    if audit_sink is not None:
        audit_sink['steps'] = steps
        audit_sink['SEM'] = SEM
        es.close()
        return nc
    with es, nc.Block() as block:
        for eng in ("sync", "gpsimd", "vector", "scalar", "tensor"):
            def body(e, _eng=eng):
                for (tag, fn) in steps:
                    if tag == _eng:
                        fn(e)
            getattr(block, eng)(body)

    return nc



def _vap_info(ap):
    t = ap.tensor
    tn = type(t).__name__
    if "SB" not in tn and "PSum" not in tn:
        return None
    dims = [tuple(x) for x in ap.ap]
    pstep, pcnt = dims[0]
    try:
        plo = ap.base_partition()
    except Exception:
        plo = 0
    coloff = ap.offset - plo * pstep if pstep else ap.offset
    span = 0
    for (s, c) in dims[1:]:
        span += s * (c - 1)
    return (t.name, plo, plo + pcnt, coloff, coloff + span + 1)


class _VRec:

    def __init__(self, sink):
        self.sink = sink

    def _aps(self, xs):
        out = []
        for x in xs:
            if hasattr(x, "tensor"):
                i = _vap_info(x)
                if i:
                    out.append(i)
        return out

    def _rec(self, reads, writes):
        self.sink.append((self._aps(reads), self._aps(writes)))
        return self

    def then_inc(self, sem, by=1):
        return self

    def wait_ge(self, sem, v):
        return self

    def reciprocal(self, out, in_):
        return self._rec([in_], [out])

    def tensor_scalar(self, out, in0, s1, s2, op0, op1=None, **kw):
        return self._rec([in0, s1, s2], [out])

    def tensor_single_scalar(self, out, in_, s, op, **kw):
        return self._rec([in_, s], [out])

    def scalar_tensor_tensor(self, out, in0, s, in1, op0, op1, **kw):
        return self._rec([in0, s, in1], [out])

    def tensor_tensor(self, out, a, b, op, **kw):
        return self._rec([a, b], [out])

    def tensor_add(self, out, a, b, **kw):
        return self._rec([a, b], [out])

    def tensor_sub(self, out, a, b, **kw):
        return self._rec([a, b], [out])

    def tensor_max(self, out, a, b, **kw):
        return self._rec([a, b], [out])

    def tensor_copy(self, out, in_, **kw):
        return self._rec([in_], [out])

    def reduce_max(self, out, in_, **kw):
        return self._rec([in_], [out])

    def reduce_sum(self, out, in_, **kw):
        return self._rec([in_], [out])

    def memset(self, ap, c, **kw):
        return self._rec([], [ap])


def _overlap(a, b):
    return (a[0] == b[0] and not (a[4] <= b[3] or b[4] <= a[3]
                                  or a[2] <= b[1] or b[2] <= a[1]))


def compute_vraw(S, ncores, **kw):
    sink = {}
    build_program(S, ncores, audit_sink=sink, **kw)
    recs = []
    rec = _VRec(recs)
    for (eng, fn) in sink["steps"]:
        if eng == "vector":
            fn(rec)
    waits = {}
    writes = []
    guaranteed = 0
    for o, (rds, wrs) in enumerate(recs):
        need = 0
        for r in rds:
            for (w, vc) in writes:
                if vc > need and _overlap(r, w):
                    need = vc
        if need > guaranteed:
            waits[o] = need
            guaranteed = need
        for w in wrs:
            writes.append((w, o + 1))
    return waits


def make_consts():
    idn = np.eye(128).astype(np.float16)
    idnf = np.eye(128).astype(np.float32)
    mT = np.triu(np.ones((128, 128))).astype(np.float16)
    ones16 = np.ones((128, 128), np.float16)
    onesf = np.ones((128, 128), np.float32)
    c2d = np.full((128, 1), 1.0 / (2 * DH), np.float16)
    three = np.full((128, 1), 3.0, np.float32)
    return idn, idnf, mT, ones16, onesf, c2d, three


def kernel(L0_attn_x, L0_attn_q_w, L0_attn_q_b, L0_attn_k_w, L0_attn_k_b,
           L0_attn_v_w, L0_attn_v_b, L0_attn_out_w, L0_attn_out_b,
           L0_attn_proj, L0_attn_gamma, L0_attn_beta,
           L0_ffn_W_in, L0_ffn_b_in, L0_ffn_W_out, L0_ffn_b_out):
    X = np.ascontiguousarray(np.asarray(L0_attn_x, np.float32))
    B, S, _ = X.shape
    assert B == NCORES
    idn, idnf, mT, ones16, onesf, c2d, three = make_consts()
    shared = {
        "q_w": np.ascontiguousarray(np.asarray(L0_attn_q_w, np.float32)),
        "k_w": np.ascontiguousarray(np.asarray(L0_attn_k_w, np.float32)),
        "v_w": np.ascontiguousarray(np.asarray(L0_attn_v_w, np.float32)),
        "out_w": np.ascontiguousarray(np.asarray(L0_attn_out_w, np.float32)),
        "q_b": np.asarray(L0_attn_q_b, np.float32).reshape(1, E),
        "k_b": np.asarray(L0_attn_k_b, np.float32).reshape(1, E),
        "v_b": np.asarray(L0_attn_v_b, np.float32).reshape(1, E),
        "out_b": np.asarray(L0_attn_out_b, np.float32).reshape(1, E),
        "projT": np.ascontiguousarray(np.asarray(L0_attn_proj, np.float32).T),
        "gamma": np.asarray(L0_attn_gamma, np.float32).reshape(1, E),
        "beta": np.asarray(L0_attn_beta, np.float32).reshape(1, E),
        "W_in": np.ascontiguousarray(np.asarray(L0_ffn_W_in, np.float32)),
        "b_in": np.asarray(L0_ffn_b_in, np.float32).reshape(1, FF),
        "W_out": np.ascontiguousarray(np.asarray(L0_ffn_W_out, np.float32)),
        "b_out": np.asarray(L0_ffn_b_out, np.float32).reshape(1, E),
        "idn16": idn, "idnf": idnf, "maskT": mT, "ones16": ones16,
        "onesf": onesf, "c2d": c2d, "three": three,
        "mln16": np.full((128, 1), -LN16, np.float32),
    }
    in_maps = [dict(shared, x=np.ascontiguousarray(X[c])) for c in range(NCORES)]
    vw = compute_vraw(S, NCORES)
    nc = build_program(S, NCORES, vraw_waits=vw)
    res = run_bass_kernel_spmd(nc, in_maps, core_ids=list(range(NCORES)))
    out = np.stack([np.asarray(res.results[c]["out"], np.float32) for c in range(NCORES)])
    return out


# version 05
# speedup vs baseline: 1.2494x; 1.2494x over previous
import math
from collections import defaultdict
from contextlib import ExitStack

import numpy as np
import ml_dtypes

import concourse.bass as bass
import concourse.mybir as mybir
from concourse.bass_utils import run_bass_kernel_spmd

F32 = mybir.dt.float32
F16 = mybir.dt.float16
AF = mybir.ActivationFunctionType
OP = mybir.AluOpType

NCORES = 8
E = 1024
FF = 4096
H = 16
DH = 64
M = 256
CH = 128
LN16 = float(np.float32(np.log(np.float32(16.0))))
LN1_EPS = 1e-5
LN2_EPS = 1e-6


def build_program(S, ncores, last_phase=4, p4_mc=None, p4_cut=None, audit_sink=None, p4_dbg=False, vraw_waits=None):
    NCH = S // CH
    MC = 256
    NMC = S // MC
    TT = 512
    NTT = S // TT
    TCG = 256
    NTCG = S // TCG

    nc = bass.Bass("TRN2", target_bir_lowering=False, debug=False,
                   detect_race_conditions=False)

    d_x = nc.dram_tensor("x", [S, E], F32, kind="ExternalInput")
    d_qw = nc.dram_tensor("q_w", [E, E], F32, kind="ExternalInput")
    d_kw = nc.dram_tensor("k_w", [E, E], F32, kind="ExternalInput")
    d_vw = nc.dram_tensor("v_w", [E, E], F32, kind="ExternalInput")
    d_ow = nc.dram_tensor("out_w", [E, E], F32, kind="ExternalInput")
    d_qb = nc.dram_tensor("q_b", [1, E], F32, kind="ExternalInput")
    d_kb = nc.dram_tensor("k_b", [1, E], F32, kind="ExternalInput")
    d_vb = nc.dram_tensor("v_b", [1, E], F32, kind="ExternalInput")
    d_ob = nc.dram_tensor("out_b", [1, E], F32, kind="ExternalInput")
    d_projT = nc.dram_tensor("projT", [DH, M], F32, kind="ExternalInput")
    d_gamma = nc.dram_tensor("gamma", [1, E], F32, kind="ExternalInput")
    d_beta = nc.dram_tensor("beta", [1, E], F32, kind="ExternalInput")
    d_win = nc.dram_tensor("W_in", [E, FF], F32, kind="ExternalInput")
    d_bin = nc.dram_tensor("b_in", [1, FF], F32, kind="ExternalInput")
    d_wout = nc.dram_tensor("W_out", [FF, E], F32, kind="ExternalInput")
    d_bout = nc.dram_tensor("b_out", [1, E], F32, kind="ExternalInput")
    d_idn16 = nc.dram_tensor("idn16", [128, 128], F16, kind="ExternalInput")
    d_idnf = nc.dram_tensor("idnf", [128, 128], F32, kind="ExternalInput")
    d_maskT = nc.dram_tensor("maskT", [128, 128], F16, kind="ExternalInput")
    d_ones16 = nc.dram_tensor("ones16", [128, 128], F16, kind="ExternalInput")
    d_onesf = nc.dram_tensor("onesf", [128, 128], F32, kind="ExternalInput")
    d_c2d = nc.dram_tensor("c2d", [128, 1], F16, kind="ExternalInput")
    d_three = nc.dram_tensor("three", [128, 1], F32, kind="ExternalInput")
    d_mln16 = nc.dram_tensor("mln16", [128, 1], F32, kind="ExternalInput")
    d_out = nc.dram_tensor("out", [S, E], F32, kind="ExternalOutput")

    dbg_kind = "ExternalOutput" if p4_dbg else "Internal"
    d_hid = nc.dram_tensor("hid_scr", [FF, S], F16, kind=dbg_kind) if p4_dbg else nc.dram_tensor("hid_scr", [FF, S], F16)
    d_g = nc.dram_tensor("g_scr", [S, E], F16, kind=dbg_kind) if p4_dbg else nc.dram_tensor("g_scr", [S, E], F16)
    d_dbg = nc.dram_tensor("dbg", [128, 64], F32, kind="ExternalOutput") if p4_dbg else None
    d_cc1i = nc.dram_tensor("cc1i", [1, 128], F32)
    d_cc1o = nc.dram_tensor("cc1o", [1, 128], F32)
    d_cc2i = nc.dram_tensor("cc2i", [1, 128], F32)
    d_cc2o = nc.dram_tensor("cc2o", [1, 128], F32)

    es = ExitStack()

    def sb(name, shape, dt):
        return es.enter_context(nc.sbuf_tensor(name, shape, dt))

    class View:
        def __init__(self, pool, base, ncols):
            self.pool, self.base, self.ncols = pool, base, ncols

        def __getitem__(self, idx):
            rows, cols = idx
            if isinstance(cols, slice):
                start = 0 if cols.start is None else cols.start
                stop = self.ncols if cols.stop is None else cols.stop
                return self.pool[rows, self.base + start:self.base + stop]
            raise TypeError(cols)

    def psum(name, shape, dt):
        return es.enter_context(nc.psum_tensor(name, shape, dt))

    xnT = sb("xnT", [128, 8 * S], F16)
    wbuf = sb("wbuf", [128, 32768], F16)
    KV32 = sb("KV32", [128, 2048], F32)
    KV16 = sb("KV16", [128, 2048], F16)
    ks32 = sb("ks32", [128, 32], F32)
    ks16 = sb("ks16", [128, 32], F16)

    idn16 = sb("idn16s", [128, 128], F16)
    idnf = sb("idnfs", [128, 128], F32)
    maskT = sb("maskTs", [128, 128], F16)
    ones16 = sb("ones16s", [128, 128], F16)
    onesf = sb("onesfs", [128, 128], F32)
    c2d = sb("c2ds", [128, 1], F16)
    cmag = sb("cmag", [128, 1], F32)
    mln16 = sb("mln16s", [128, 1], F32)
    projs = sb("projs", [128, 256], F16)
    qbT = sb("qbT", [128, 8], F32)
    kbT = sb("kbT", [128, 8], F32)
    binT = sb("binT", [128, 32], F32)
    rows16 = sb("rows16", [128, E], F16)

    amaxr = sb("amaxr", [128, 1], F32)
    hmaxr = sb("hmaxr", [128, 1], F32)
    sc = sb("scl", [128, 12], F32)
    amaxf = sb("amaxf", [128, 32], F32)
    sw1 = sb("sw1", [128, 32], F32)
    invwx = sb("invwx", [128, 32], F32)

    S16 = sb("S16pool", [128, 16128], F16)
    S32 = sb("S32pool", [128, 4096], F32)
    xb = [sb("xb%d" % i, [128, E], F32) for i in range(2)]
    x2t = xb
    st1t = sb("st1", [128, 16], F32)
    mub = [sb("mub%d" % i, [128, 1], F32) for i in range(2)]
    rb = [sb("rb%d" % i, [128, 1], F32) for i in range(2)]
    xntok = [View(S16, 0, E), View(S16, E, E)]
    sqb = View(S16, 2048, E)
    winb = [View(S16, 0, FF), View(S16, FF, FF)]
    wmax = View(S16, 2 * FF, FF)
    scb = View(S16, 2 * FF, FF)
    xsim = [View(S16, 0, 8 * TT), View(S16, 4096, 8 * TT)]
    sty = View(S32, 0, FF)
    stq = View(S32, 0, TT)
    hstg = [View(S16, 8192 + i * TT, TT) for i in range(4)]
    hred = sb("hred", [128, 2], F32)
    wob = [View(S32, 1024, E), View(S32, 2048, E)]
    wos = sb("wos", [128, 8], F32)
    sty2 = View(S32, 0, E)
    bor32v = View(S32, 3072, E)
    gstg = [View(S16, 0, E), View(S16, E, E)]
    hrl = [View(S16, 3072 + i * 512, 512) for i in range(4)]
    ihq = [View(S16, 5120 + i * 512, 512) for i in range(2)]
    stq3 = View(S32, 0, 512)
    QT = View(S16, 0, 8 * MC)
    KT = View(S16, 8 * MC, 8 * MC)
    ksq = View(S16, 16 * MC, 8 * MC)
    Vsb = View(S16, 24 * MC, 2 * E)
    atok = View(S16, 8192, E)
    atT = View(S16, 9216, E)
    sq2 = View(S16, 8192, E)
    g2t = [View(S16, 10240, E), View(S16, 11264, E)]
    QPb = [View(S16, 12288, 256), View(S16, 12544, 256)]
    KPb = [View(S16, 12800, 256), View(S16, 13056, 256)]
    kptb = [View(S16, 13312, 256), View(S16, 13568, 256)]
    Am16 = [View(S16, 13824, 128), View(S16, 13952, 128)]
    kpbias = sb("kpbias", [128, 16], F32)
    eckb = sb("eckb", [128, 16], F32)
    recipb = sb("recipb", [128, 16], F32)
    ab = [View(S32, 0, E), View(S32, 1024, E)]
    ab2 = ab
    ln2s = sb("ln2s", [128, 8], F32)
    gammab = View(S16, 14080, E)
    betab = View(S16, 15104, E)
    ccst = View(S32, 3072, 128)
    rhsD = View(S32, 1024, 128)
    ccst2 = View(S32, 3200, 128)
    projf = View(S32, 3328, 256)

    PSA = psum("PSA", [128, 1024], F32)
    PSB = psum("PSB", [128, 1024], F32)
    PSC = psum("PSC", [128, 512], F32)
    PSD = psum("PSD", [128, 512], F32)
    PSE = psum("PSE", [128, 512], F32)
    PST = psum("PST", [128, 1024], F16)

    sem_names = ["xl", "wl", "gw", "hs", "hr", "gs", "ccd", "cc", "xr", "gl",
                 "out", "T", "V", "SC", "G"]
    SEM = {n: es.enter_context(nc.semaphore("s_" + n)) for n in sem_names}

    steps = []
    cnt = defaultdict(int)
    vord = [0]

    def on(eng, fn):
        steps.append((eng, fn))

    def W(eng, s, v=None):
        t = cnt[s] if v is None else v
        assert t <= cnt[s], (s, t, cnt[s])
        if t > 0:
            on(eng, lambda e, _t=t, _s=s: e.wait_ge(SEM[_s], _t))

    def mark(s, by=1):
        cnt[s] += by
        return cnt[s]

    def E_(eng, s, by, fn):
        if eng == 'vector':
            o = vord[0]
            vord[0] += 1
            wv = vraw_waits.get(o) if vraw_waits is not None else None
            if wv:
                on('vector', lambda e, _t=wv: e.wait_ge(SEM['V'], _t))
            if s == 'V' and by == 1:
                mark('V', 1)
                on(eng, lambda e, _fn=fn: _fn(e).then_inc(SEM['V'], 1))
            else:
                raise AssertionError("vector ops must inc V")
            return
        mark(s, by)
        on(eng, lambda e, _fn=fn, _s=s, _by=by: _fn(e).then_inc(SEM[_s], _by))

    def Eq(eng, fn):
        if eng == 'vector':
            E_('vector', 'V', 1, fn)
            return
        on(eng, lambda e, _fn=fn: _fn(e))

    for (dst, src) in [(idn16, d_idn16), (idnf, d_idnf), (maskT, d_maskT),
                       (ones16, d_ones16), (onesf, d_onesf), (c2d, d_c2d),
                       (cmag, d_three), (mln16, d_mln16)]:
        E_('sync', 'wl', 16, lambda e, _d=dst, _s=src: e.dma_start(out=_d[:, :], in_=_s[:, :]))
    E_('sync', 'wl', 16, lambda e: e.dma_start(out=projf[0:64, :], in_=d_projT[:, :]))
    E_('sync', 'wl', 16, lambda e: e.dma_start(out=projf[64:128, :], in_=d_projT[:, :]))
    E_('sync', 'wl', 16, lambda e: e.dma_start(
        out=qbT[:, :], in_=bass.AP(d_qb, 0, [[1, 128], [128, 8]]),
        allow_slow_non_contiguous=True))
    E_('sync', 'wl', 16, lambda e: e.dma_start(
        out=kbT[:, :], in_=bass.AP(d_kb, 0, [[1, 128], [128, 8]]),
        allow_slow_non_contiguous=True))
    E_('sync', 'wl', 16, lambda e: e.dma_start(
        out=binT[:, :], in_=bass.AP(d_bin, 0, [[1, 128], [128, 32]]),
        allow_slow_non_contiguous=True))
    E_('gpsimd', 'gw', 16, lambda e: e.dma_start(out=rows16[0:1, :], in_=d_gamma[:, :]))
    E_('gpsimd', 'gw', 16, lambda e: e.dma_start(out=rows16[32:33, :], in_=d_beta[:, :]))

    wl_p0 = cnt['wl']
    gw_p0 = cnt['gw']

    W('vector', 'wl', wl_p0)
    Eq('vector', lambda e: e.reciprocal(cmag[:, :], cmag[:, :]))
    Eq('vector', lambda e: e.tensor_scalar(cmag[:, :], cmag[:, :], 4.0, 1.0, OP.mult, OP.subtract))
    Eq('vector', lambda e: e.tensor_scalar(cmag[:, :], cmag[:, :], 3.0, 1.0, OP.mult, OP.subtract))
    Eq('vector', lambda e: e.reciprocal(cmag[:, :], cmag[:, :]))
    E_('vector', 'V', 1, lambda e: e.tensor_single_scalar(cmag[:, :], cmag[:, :], 1.5, OP.mult))
    v_cmag = cnt['V']

    W('scalar', 'wl', wl_p0)
    E_('scalar', 'SC', 1, lambda e: e.activation(projs[:, :], projf[:, :], AF.Copy,
                                                 scale=1.0 / math.sqrt(DH)))
    W('tensor', 'wl', wl_p0)
    W('tensor', 'gw', gw_p0)
    E_('tensor', 'T', 1, lambda e: e.matmul(PSC[:, 0:512], ones16[0:1, 0:128], rows16[0:1, 0:512], start=True, stop=True))
    E_('tensor', 'T', 1, lambda e: e.matmul(PSD[:, 0:512], ones16[0:1, 0:128], rows16[0:1, 512:1024], start=True, stop=True))
    t_g1 = cnt['T']
    W('scalar', 'T', t_g1)
    E_('scalar', 'SC', 1, lambda e: e.activation(gammab[:, 0:512], PSC[:, 0:512], AF.Copy))
    E_('scalar', 'SC', 1, lambda e: e.activation(gammab[:, 512:1024], PSD[:, 0:512], AF.Copy))
    sc_g1 = cnt['SC']
    W('tensor', 'SC', sc_g1)
    E_('tensor', 'T', 1, lambda e: e.matmul(PSC[:, 0:512], ones16[32:33, 0:128], rows16[32:33, 0:512], start=True, stop=True))
    E_('tensor', 'T', 1, lambda e: e.matmul(PSD[:, 0:512], ones16[32:33, 0:128], rows16[32:33, 512:1024], start=True, stop=True))
    t_g2 = cnt['T']
    W('scalar', 'T', t_g2)
    E_('scalar', 'SC', 1, lambda e: e.activation(betab[:, 0:512], PSC[:, 0:512], AF.Copy))
    E_('scalar', 'SC', 1, lambda e: e.activation(betab[:, 512:1024], PSD[:, 0:512], AF.Copy))
    sc_g2 = cnt['SC']
    W('gpsimd', 'T', t_g2)
    E_('gpsimd', 'gw', 16, lambda e: e.dma_start(out=rows16[0:1, :], in_=d_vb[:, :]))
    E_('gpsimd', 'gw', 16, lambda e: e.dma_start(out=rows16[32:33, :], in_=d_ob[:, :]))

    v_xn = {}
    sc_ev = {}

    class _Off:
        def __init__(self, t, off):
            self.t, self.off = t, off

        def __getitem__(self, idx):
            rows, cols = idx
            return self.t[rows, cols.start + self.off:cols.stop + self.off]
    for ci in range(NCH):
        st1 = _Off(st1t, 8 * (ci % 2))
        if ci >= 2:
            W('sync', 'V', v_xn[ci - 2])
        E_('sync', 'xl', 16, lambda e, _ci=ci: e.dma_start(
            out=xb[_ci % 2][:, :], in_=d_x[_ci * CH:(_ci + 1) * CH, :]))
        W('scalar', 'xl', 16 * (ci + 1))
        if ci >= 1:
            W('scalar', 'V', v_xn[ci - 1])
        E_('scalar', 'SC', 1, lambda e, _ci=ci: e.activation(sqb[:, :], xb[_ci % 2][:, :], AF.Square))
        sc_sq = cnt['SC']
        W('vector', 'xl', 16 * (ci + 1))
        Eq('vector', lambda e, _ci=ci: e.reduce_sum(st1[:, 0:1], xb[_ci % 2][:, :], axis=mybir.AxisListType.X))
        W('vector', 'SC', sc_sq)
        Eq('vector', lambda e: e.reduce_sum(st1[:, 1:2], sqb[:, :], axis=mybir.AxisListType.X))
        Eq('vector', lambda e, _ci=ci: e.tensor_single_scalar(mub[_ci % 2][:, :], st1[:, 0:1], 1.0 / E, OP.mult))
        Eq('vector', lambda e: e.tensor_single_scalar(st1[:, 2:3], st1[:, 1:2], 1.0 / E, OP.mult))
        Eq('vector', lambda e, _ci=ci: e.tensor_single_scalar(st1[:, 3:4], mub[_ci % 2][:, :], mub[_ci % 2][:, 0:1], OP.mult))
        Eq('vector', lambda e: e.tensor_sub(st1[:, 4:5], st1[:, 2:3], st1[:, 3:4]))
        Eq('vector', lambda e: e.tensor_single_scalar(st1[:, 4:5], st1[:, 4:5], LN1_EPS, OP.add))
        E_('vector', 'V', 1, lambda e: e.reciprocal(st1[:, 5:6], st1[:, 4:5]))
        v_vr = cnt['V']
        W('scalar', 'V', v_vr)
        E_('scalar', 'SC', 1, lambda e, _ci=ci: e.activation(rb[_ci % 2][:, :], st1[:, 5:6], AF.Sqrt))
        sc_r = cnt['SC']
        W('vector', 'SC', sc_r)
        Eq('vector', lambda e, _ci=ci: e.tensor_scalar(
            xntok[_ci % 2][:, :], xb[_ci % 2][:, :], mub[_ci % 2][:, 0:1], rb[_ci % 2][:, 0:1],
            OP.subtract, OP.mult))
        Eq('vector', lambda e, _ci=ci: e.reduce_max(st1[:, 6:7], xntok[_ci % 2][:, :],
                                                    axis=mybir.AxisListType.X, apply_absolute_value=True))
        if ci == 0:
            E_('vector', 'V', 1, lambda e: e.tensor_copy(amaxr[:, :], st1[:, 6:7]))
        else:
            E_('vector', 'V', 1, lambda e: e.tensor_max(amaxr[:, :], amaxr[:, :], st1[:, 6:7]))
        v_xn[ci] = cnt['V']
        W('tensor', 'V', v_xn[ci])
        if ci >= 1:
            W('tensor', 'SC', sc_ev[ci - 1])
        for b in range(8):
            if b < 7:
                Eq('tensor', lambda e, _ci=ci, _b=b: e.matmul(
                    PST[:, _b * 128:(_b + 1) * 128], xntok[_ci % 2][:, _b * 128:(_b + 1) * 128],
                    idn16[:, :], is_transpose=True, start=True, stop=True))
            else:
                E_('tensor', 'T', 1, lambda e, _ci=ci, _b=b: e.matmul(
                    PST[:, _b * 128:(_b + 1) * 128], xntok[_ci % 2][:, _b * 128:(_b + 1) * 128],
                    idn16[:, :], is_transpose=True, start=True, stop=True))
        t_tr = cnt['T']
        W('scalar', 'T', t_tr)
        for b in range(8):
            if b < 7:
                Eq('scalar', lambda e, _ci=ci, _b=b: e.activation(
                    xnT[:, _b * S + _ci * CH: _b * S + (_ci + 1) * CH],
                    PST[:, _b * 128:(_b + 1) * 128], AF.Copy))
            else:
                E_('scalar', 'SC', 1, lambda e, _ci=ci, _b=b: e.activation(
                    xnT[:, _b * S + _ci * CH: _b * S + (_ci + 1) * CH],
                    PST[:, _b * 128:(_b + 1) * 128], AF.Copy))
        sc_ev[ci] = cnt['SC']

    st1 = _Off(st1t, 0)
    def amax_collective(src_run, cc_in, cc_out, col_amax, col_s, col_inv, qscale):
        W('tensor', 'V')
        E_('tensor', 'T', 1, lambda e: e.matmul(PSD[0:1, 0:128], src_run[:, 0:1], idnf[:, :],
                                                is_transpose=True, start=True, stop=True))
        W('vector', 'T')
        E_('vector', 'V', 1, lambda e: e.tensor_copy(ccst[0:1, 0:128], PSD[0:1, 0:128]))
        W('gpsimd', 'V')
        E_('gpsimd', 'ccd', 16, lambda e: e.dma_start(out=cc_in[0:1, 0:128], in_=ccst[0:1, 0:128]))
        W('gpsimd', 'ccd')
        E_('gpsimd', 'cc', 1, lambda e: e.collective_compute(
            "AllReduce", OP.max, replica_groups=[list(range(ncores))],
            ins=[cc_in.ap().opt()], outs=[cc_out.ap().opt()]))
        W('gpsimd', 'cc')
        E_('gpsimd', 'ccd', 16, lambda e: e.dma_start(out=ccst2[0:1, 0:128], in_=cc_out[0:1, 0:128]))
        W('vector', 'ccd')
        W('vector', 'cc')
        Eq('vector', lambda e: e.reduce_max(ccst[0:1, 0:1], ccst2[0:1, 0:128], axis=mybir.AxisListType.X))
        E_('vector', 'V', 1, lambda e: e.tensor_single_scalar(ccst[0:1, 0:1], ccst[0:1, 0:1], 1e-8, OP.max))
        W('tensor', 'V')
        E_('tensor', 'T', 1, lambda e: e.matmul(PSD[0:128, 128:129], onesf[0:1, 0:128],
                                                ccst[0:1, 0:1], start=True, stop=True))
        W('vector', 'T')
        Eq('vector', lambda e: e.tensor_copy(sc[:, col_amax:col_amax + 1], PSD[:, 128:129]))
        Eq('vector', lambda e: e.reciprocal(sc[:, 6:7], sc[:, col_amax:col_amax + 1]))
        Eq('vector', lambda e, _q=qscale: e.tensor_single_scalar(sc[:, col_s:col_s + 1], sc[:, 6:7], _q, OP.mult))
        E_('vector', 'V', 1, lambda e: e.reciprocal(sc[:, col_inv:col_inv + 1], sc[:, col_s:col_s + 1]))

    amax_collective(amaxr, d_cc1i, d_cc1o, 0, 1, 2, 32767.0)
    v_scales_x = cnt['V']

    for eb in range(8):
        if eb == 0:
            W('gpsimd', 'V', cnt['V'])
        else:
            W('gpsimd', 'SC')
        E_('gpsimd', 'gw', 16, lambda e, _eb=eb: e.dma_start(
            out=winb[0][:, :], in_=d_win[_eb * 128:(_eb + 1) * 128, :]))
        W('scalar', 'gw', cnt['gw'])
        if eb >= 2:
            W('scalar', 'V')
        E_('scalar', 'SC', 1, lambda e: e.activation(winb[1][:, :], winb[0][:, :], AF.Abs))
        W('vector', 'SC', cnt['SC'])
        if eb == 0:
            E_('vector', 'V', 1, lambda e: e.tensor_copy(wmax[:, :], winb[1][:, :]))
        else:
            E_('vector', 'V', 1, lambda e: e.tensor_tensor(
                wmax[:, :], wmax[:, :], winb[1][:, :], OP.max))
    v_wmax = cnt['V']
    for rnd in range(4):
        W('tensor', 'V', v_wmax)
        if rnd == 0:
            W('tensor', 'SC', sc_ev[NCH - 1])
        if rnd >= 1:
            W('tensor', 'V')
        for j in range(8):
            fb = rnd * 8 + j
            fn = lambda e, _fb=fb, _j=j: e.matmul(
                PST[:, _j * 128:(_j + 1) * 128], wmax[:, _fb * 128:(_fb + 1) * 128],
                idn16[:, :], is_transpose=True, start=True, stop=True)
            if j < 7:
                Eq('tensor', fn)
            else:
                E_('tensor', 'T', 1, fn)
        W('vector', 'T', cnt['T'])
        for j in range(8):
            fb = rnd * 8 + j
            fn = lambda e, _fb=fb, _j=j: e.reduce_max(
                amaxf[:, _fb:_fb + 1], PST[:, _j * 128:(_j + 1) * 128], axis=mybir.AxisListType.X)
            if j < 7:
                Eq('vector', fn)
            else:
                E_('vector', 'V', 1, fn)
    W('vector', 'V')
    Eq('vector', lambda e: e.tensor_single_scalar(amaxf[:, :], amaxf[:, :], 1e-8, OP.max))
    Eq('vector', lambda e: e.reciprocal(sw1[:, :], amaxf[:, :]))
    Eq('vector', lambda e: e.tensor_single_scalar(sw1[:, :], sw1[:, :], 127.0, OP.mult))
    Eq('vector', lambda e: e.reciprocal(invwx[:, :], sw1[:, :]))
    W('vector', 'V', v_scales_x)
    E_('vector', 'V', 1, lambda e: e.tensor_single_scalar(invwx[:, :], invwx[:, :], sc[:, 2:3], OP.mult))
    v_sw1 = cnt['V']
    for grp in range(8):
        for j in range(4):
            b = grp * 4 + j
            fn = lambda e, _b=b: e.tensor_scalar(
                rhsD[:, 0:128], idnf[:, :], sw1[:, _b:_b + 1], None, OP.mult)
            if b == 0:
                W('vector', 'V', v_sw1)
            else:
                W('vector', 'T')
            E_('vector', 'V', 1, fn)
            W('tensor', 'V', cnt['V'])
            if grp >= 1 and j == 0:
                W('tensor', 'SC')
            fn2 = lambda e, _j=j: e.matmul(
                PSD[:, _j * 128:(_j + 1) * 128], onesf[:, 0:128], rhsD[:, 0:128],
                start=True, stop=True)
            E_('tensor', 'T', 1, fn2)
        W('scalar', 'T', cnt['T'])
        fn = lambda e, _g=grp: e.activation(scb[:, _g * 512:(_g + 1) * 512], PSD[:, 0:512], AF.Copy)
        E_('scalar', 'SC', 1, fn)
    sc_scb = cnt['SC']
    W('vector', 'SC', sc_scb)
    W('vector', 'V', v_cmag)
    for eb in range(8):
        if eb >= 2:
            W('gpsimd', 'V', cnt['V'])
        else:
            W('gpsimd', 'V', v_wmax)
        E_('gpsimd', 'gw', 16, lambda e, _eb=eb: e.dma_start(
            out=winb[_eb % 2][:, :], in_=d_win[_eb * 128:(_eb + 1) * 128, :]))
        W('vector', 'gw', cnt['gw'])
        Eq('vector', lambda e, _eb=eb: e.tensor_tensor(sty[:, :], winb[_eb % 2][:, :], scb[:, :], OP.mult))
        E_('vector', 'V', 1, lambda e, _eb=eb: e.tensor_scalar(
            wbuf[:, _eb * FF:(_eb + 1) * FF], sty[:, :], cmag[:, 0:1], cmag[:, 0:1], OP.add, OP.subtract))
    v_win_ready = cnt['V']

    sc_gelu = {}
    g_hdone = {}
    t_ffn1 = {}
    for tt in range(NTT):
        W('vector', 'V', v_scales_x)
        if tt >= 2:
            W('tensor', 'V')
        if tt >= 2:
            W('vector', 'T', t_ffn1[tt - 2])
        for eb in range(8):
            Eq('vector', lambda e, _tt=tt, _eb=eb: e.tensor_scalar(
                stq[:, :], xnT[:, _eb * S + _tt * TT: _eb * S + (_tt + 1) * TT],
                sc[:, 1:2], cmag[:, 0:1], OP.mult, OP.add))
            fn = lambda e, _tt=tt, _eb=eb: e.tensor_scalar(
                xsim[_tt % 2][:, _eb * TT:(_eb + 1) * TT], stq[:, :], cmag[:, 0:1], None, OP.subtract)
            if eb < 7:
                Eq('vector', fn)
            else:
                E_('vector', 'V', 1, fn)
        v_xsim = cnt['V']
        W('tensor', 'V', v_xsim)
        if tt == 0:
            W('tensor', 'V', v_win_ready)
        for fb in range(32):
            ps = PSC if fb % 2 == 0 else PSD
            if fb >= 2:
                W('tensor', 'SC', sc_gelu[(tt, fb - 2)])
            elif tt >= 1:
                W('tensor', 'SC', sc_gelu[(tt - 1, fb + 30)])
            for kb in range(8):
                fn = lambda e, _ps=ps, _fb=fb, _kb=kb, _tt=tt: e.matmul(
                    _ps[:, 0:TT], wbuf[:, _kb * FF + _fb * 128: _kb * FF + (_fb + 1) * 128],
                    xsim[_tt % 2][:, _kb * TT:(_kb + 1) * TT],
                    start=(_kb == 0), stop=(_kb == 7))
                if kb < 7:
                    Eq('tensor', fn)
                else:
                    E_('tensor', 'T', 1, fn)
            t_mm = cnt['T']
            W('scalar', 'T', t_mm)
            if (tt * 32 + fb) >= 4:
                pi = (tt * 32 + fb) - 4
                W('scalar', 'V', g_hdone.get(pi, 0))
                W('scalar', 'hs', 16 * (pi + 1))
            E_('scalar', 'SC', 1, lambda e, _ps=ps, _fb=fb: e.activation(
                hstg[_fb % 4][:, :], _ps[:, 0:TT], AF.Gelu,
                bias=binT[:, _fb:_fb + 1], scale=invwx[:, _fb:_fb + 1]))
            sc_gelu[(tt, fb)] = cnt['SC']
            W('vector', 'SC', cnt['SC'])
            Eq('vector', lambda e, _fb=fb: e.reduce_max(
                hred[:, 0:1], hstg[_fb % 4][:, :], axis=mybir.AxisListType.X,
                apply_absolute_value=True))
            if tt == 0 and fb == 0:
                E_('vector', 'V', 1, lambda e: e.tensor_copy(hmaxr[:, :], hred[:, 0:1]))
            else:
                E_('vector', 'V', 1, lambda e: e.tensor_max(hmaxr[:, :], hmaxr[:, :], hred[:, 0:1]))
            g_hdone[tt * 32 + fb] = cnt['V']
            W('gpsimd', 'SC', cnt['SC'])
            E_('gpsimd', 'hs', 16, lambda e, _tt=tt, _fb=fb: e.dma_start(
                out=d_hid[_fb * 128:(_fb + 1) * 128, _tt * TT:(_tt + 1) * TT],
                in_=hstg[_fb % 4][:, :]))
        t_ffn1[tt] = cnt['T']

    if last_phase < 3:
        W('sync', 'hs', cnt['hs'])
        if audit_sink is not None:
            audit_sink['steps'] = steps; audit_sink['SEM'] = SEM; es.close(); return nc
        with es, nc.Block() as block:
            for eng in ("sync", "gpsimd", "vector", "scalar", "tensor"):
                def body(e, _eng=eng):
                    for (tag, fn) in steps:
                        if tag == _eng:
                            fn(e)
                getattr(block, eng)(body)
        return nc
    amax_collective(hmaxr, d_cc2i, d_cc2o, 3, 4, 5, 32767.0)
    v_scales_h = cnt['V']
    W('sync', 'V', cnt['V'])
    E_('sync', 'xr', 16, lambda e: e.dma_start(out=bor32v[0:1, :], in_=d_bout[:, :]))
    W('vector', 'xr', cnt['xr'])
    E_('vector', 'V', 1, lambda e: e.tensor_single_scalar(rows16[64:65, :], bor32v[0:1, :], sc[64:65, 4:5], OP.mult))
    v_bor = cnt['V']

    sc_wo_start = cnt['SC']
    W('sync', 'V', cnt['V'])
    for fb in range(32):
        if fb >= 2:
            W('sync', 'V')
        E_('sync', 'xr', 16, lambda e, _fb=fb: e.dma_start(
            out=wob[_fb % 2][:, :], in_=d_wout[_fb * 128:(_fb + 1) * 128, :]))
        W('vector', 'xr', cnt['xr'])
        W('vector', 'T', t_ffn1[NTT - 1])
        Eq('vector', lambda e, _fb=fb: e.reduce_max(
            wos[:, 0:1], wob[_fb % 2][:, :], axis=mybir.AxisListType.X, apply_absolute_value=True))
        Eq('vector', lambda e: e.tensor_single_scalar(wos[:, 0:1], wos[:, 0:1], 1e-8, OP.max))
        Eq('vector', lambda e: e.reciprocal(wos[:, 1:2], wos[:, 0:1]))
        Eq('vector', lambda e: e.tensor_single_scalar(wos[:, 1:2], wos[:, 1:2], 127.0, OP.mult))
        Eq('vector', lambda e: e.reciprocal(wos[:, 2:3], wos[:, 1:2]))
        Eq('vector', lambda e, _fb=fb: e.tensor_scalar(
            sty2[:, :], wob[_fb % 2][:, :], wos[:, 1:2], cmag[:, 0:1], OP.mult, OP.add))
        E_('vector', 'V', 1, lambda e, _fb=fb: e.tensor_scalar(
            wbuf[:, _fb * E:(_fb + 1) * E], sty2[:, :], cmag[:, 0:1], wos[:, 2:3],
            OP.subtract, OP.mult))
    v_wout_ready = cnt['V']

    sc_silu = {}
    t_ffn2 = {}
    g_gst = {}
    W('gpsimd', 'T', t_ffn1[NTT - 1])
    W('gpsimd', 'hs', cnt['hs'])
    v_quant = {}
    for tg in range(NTCG):
        for fbp in range(16):
            idx = tg * 16 + fbp
            if idx >= 4:
                W('gpsimd', 'V', v_quant.get(idx - 4, 0))
            E_('gpsimd', 'hr', 16, lambda e, _tg=tg, _fbp=fbp: e.dma_start(
                out=hrl[_fbp % 4][:, :],
                in_=bass.AP(d_hid, (_fbp * 256) * S + _tg * TCG,
                            [[S, 128], [128 * S, 2], [1, TCG]])))
            W('vector', 'hr', cnt['hr'])
            W('vector', 'V', v_scales_h)
            if idx >= 2:
                W('vector', 'T', t_ffn2.get(idx - 2, 0))
            Eq('vector', lambda e, _fbp=fbp: e.tensor_scalar(
                stq3[:, :], hrl[_fbp % 4][:, :], sc[:, 4:5], cmag[:, 0:1], OP.mult, OP.add))
            E_('vector', 'V', 1, lambda e, _fbp=fbp: e.tensor_scalar(
                ihq[_fbp % 2][:, :], stq3[:, :], cmag[:, 0:1], None, OP.subtract))
            v_quant[idx] = cnt['V']
            W('tensor', 'V', cnt['V'])
            if fbp == 0:
                W('tensor', 'V', v_wout_ready)
                if tg >= 1:
                    W('tensor', 'SC', sc_silu[(tg - 1, 1)])
            for j in range(2):
                fb = 2 * fbp + j
                for tc in range(2):
                    PS = PSA if tc == 0 else PSB
                    for hf in range(2):
                        fn = lambda e, _PS=PS, _fb=fb, _j=j, _tc=tc, _hf=hf, _fbp=fbp: e.matmul(
                            _PS[:, _hf * 512:(_hf + 1) * 512],
                            ihq[_fbp % 2][:, _j * 256 + _tc * 128: _j * 256 + (_tc + 1) * 128],
                            wbuf[:, _fb * E + _hf * 512: _fb * E + (_hf + 1) * 512],
                            start=(_fb == 0), stop=False)
                        if j == 1 and tc == 1 and hf == 1:
                            E_('tensor', 'T', 1, fn)
                        else:
                            Eq('tensor', fn)
            t_ffn2[idx] = cnt['T']
        W('tensor', 'V', v_bor)
        for tc in range(2):
            PS = PSA if tc == 0 else PSB
            for hf in range(2):
                fn = lambda e, _PS=PS, _tc=tc, _hf=hf: e.matmul(
                    _PS[:, _hf * 512:(_hf + 1) * 512], ones16[64:65, 0:128],
                    rows16[64:65, _hf * 512:(_hf + 1) * 512], start=False, stop=True)
                if hf == 1:
                    E_('tensor', 'T', 1, fn)
                else:
                    Eq('tensor', fn)
            t_bias = cnt['T']
            W('scalar', 'T', t_bias)
            if tg >= 1:
                W('scalar', 'gs', 16 * (2 * (tg - 1) + tc + 1))
            E_('scalar', 'SC', 1, lambda e, _PS=PS, _tc=tc: e.activation(
                gstg[_tc][:, :], _PS[:, 0:1024], AF.Silu, scale=sc[:, 5:6]))
            sc_silu[(tg, tc)] = cnt['SC']
            W('gpsimd', 'SC', cnt['SC'])
            E_('gpsimd', 'gs', 16, lambda e, _tg=tg, _tc=tc: e.dma_start(
                out=d_g[_tg * TCG + _tc * 128: _tg * TCG + (_tc + 1) * 128, :],
                in_=gstg[_tc][:, :]))

    if last_phase < 4:
        W('sync', 'gs', cnt['gs'])
        if audit_sink is not None:
            audit_sink['steps'] = steps; audit_sink['SEM'] = SEM; es.close(); return nc
        with es, nc.Block() as block:
            for eng in ("sync", "gpsimd", "vector", "scalar", "tensor"):
                def body(e, _eng=eng):
                    for (tag, fn) in steps:
                        if tag == _eng:
                            fn(e)
                getattr(block, eng)(body)
        return nc
    t_last_ffn2 = cnt['T']
    W('gpsimd', 'T', t_last_ffn2)
    for wi, dw in enumerate([d_qw, d_kw, d_vw, d_ow]):
        for eb in range(8):
            fn = lambda e, _wi=wi, _eb=eb, _dw=dw: e.dma_start(
                out=wbuf[:, _wi * 8192 + _eb * 1024: _wi * 8192 + (_eb + 1) * 1024],
                in_=_dw[_eb * 128:(_eb + 1) * 128, :])
            E_('gpsimd', 'gl', 16, fn)
    gl_w = cnt['gl']
    W('sync', 'gs', cnt['gs'])
    W('scalar', 'gs', cnt['gs'])
    W('scalar', 'V', cnt['V'])
    QWO, KWO, VWO, OWO = 0, 8192, 16384, 24576

    E_('gpsimd', 'G', 1, lambda e: e.memset(KV32[:, :], 0.0))
    E_('gpsimd', 'G', 1, lambda e: e.memset(KV16[:, :], 0.0))
    E_('gpsimd', 'G', 1, lambda e: e.memset(ks32[:, :], 0.0))
    E_('gpsimd', 'G', 1, lambda e: e.memset(ks16[:, :], 0.0))
    g_state0 = cnt['G']

    if p4_mc is not None:
        _NMC_RUN = p4_mc
    else:
        _NMC_RUN = NMC
    v_state = 0
    g_state = g_state0
    sc_out_prev = {}
    v_ab_done = {}
    for mc in range(_NMC_RUN):
        W('tensor', 'gl', gl_w)
        W('tensor', 'SC', cnt['SC'])
        W('tensor', 'V', cnt['V'])
        for which, (WO, dst, bia) in enumerate([(QWO, QT, qbT), (KWO, KT, kbT)]):
            for b in range(8):
                PS = PSA if b < 4 else PSB
                off = (b % 4) * 256
                for kb in range(8):
                    fn = lambda e, _PS=PS, _off=off, _WO=WO, _b=b, _kb=kb, _mc=mc: e.matmul(
                        _PS[:, _off:_off + MC],
                        wbuf[:, _WO + _kb * 1024 + _b * 128: _WO + _kb * 1024 + (_b + 1) * 128],
                        xnT[:, _kb * S + _mc * MC: _kb * S + (_mc + 1) * MC],
                        start=(_kb == 0), stop=(_kb == 7))
                    if kb < 7:
                        Eq('tensor', fn)
                    else:
                        E_('tensor', 'T', 1, fn)
            t_qk = cnt['T']
            W('scalar', 'T', t_qk)
            for b in range(8):
                PS = PSA if b < 4 else PSB
                off = (b % 4) * 256
                fn = lambda e, _PS=PS, _off=off, _b=b, _dst=dst, _bia=bia: e.activation(
                    _dst[:, _b * MC:(_b + 1) * MC], _PS[:, _off:_off + MC], AF.Identity,
                    bias=_bia[:, _b:_b + 1])
                if b < 7:
                    Eq('scalar', fn)
                else:
                    E_('scalar', 'SC', 1, fn)
            sc_qk = cnt['SC']
            W('tensor', 'SC', sc_qk)
        for b in range(8):
            fn = lambda e, _b=b: e.activation(ksq[:, _b * MC:(_b + 1) * MC],
                                              KT[:, _b * MC:(_b + 1) * MC], AF.Square)
            if b < 7:
                Eq('scalar', fn)
            else:
                E_('scalar', 'SC', 1, fn)
        sc_ksq = cnt['SC']
        for tc in range(2):
            PS = PSA if tc == 0 else PSB
            for hf in range(2):
                for kb in range(8):
                    Eq('tensor', lambda e, _PS=PS, _hf=hf, _kb=kb, _mc=mc, _tc=tc: e.matmul(
                        _PS[:, _hf * 512:(_hf + 1) * 512],
                        xnT[:, _kb * S + (_mc * 2 + _tc) * 128: _kb * S + (_mc * 2 + _tc + 1) * 128],
                        wbuf[:, VWO + _kb * 1024 + _hf * 512: VWO + _kb * 1024 + (_hf + 1) * 512],
                        start=(_kb == 0), stop=False))
                fn = lambda e, _PS=PS, _hf=hf: e.matmul(
                    _PS[:, _hf * 512:(_hf + 1) * 512], ones16[0:1, 0:128],
                    rows16[0:1, _hf * 512:(_hf + 1) * 512], start=False, stop=True)
                if hf == 1 and tc == 1:
                    E_('tensor', 'T', 1, fn)
                else:
                    Eq('tensor', fn)
        t_v = cnt['T']
        W('scalar', 'T', t_v)
        for tc in range(2):
            PS = PSA if tc == 0 else PSB
            fn = lambda e, _PS=PS, _tc=tc: e.activation(Vsb[:, _tc * E:(_tc + 1) * E],
                                                        _PS[:, 0:1024], AF.Copy)
            if tc == 1:
                E_('scalar', 'SC', 1, fn)
            else:
                Eq('scalar', fn)
        sc_v = cnt['SC']
        if p4_cut == 'qkv':
            break

        for tci in range(2):
            ci = mc * 2 + tci
            tok0 = tci * 128
            W('tensor', 'SC', sc_v)
            W('tensor', 'SC', cnt['SC'])
            W('tensor', 'V', cnt['V'])
            for h in range(16):
                po = 64 * (h % 2)
                co = (h // 2) * MC + tok0
                fn = lambda e, _po=po, _co=co, _h=h: e.matmul(
                    PSD[:, 256 + _h:256 + _h + 1], ksq[_po:_po + 64, _co:_co + 128],
                    c2d[_po:_po + 64, 0:1], start=True, stop=True)
                if h < 15:
                    Eq('tensor', fn)
                else:
                    E_('tensor', 'T', 1, fn)
            t_ck = cnt['T']
            W('vector', 'T', t_ck)
            E_('vector', 'V', 1, lambda e: e.tensor_scalar(
                kpbias[:, :], PSD[:, 256:272], -1.0, -LN16, OP.mult, OP.add))
            v_kpb = cnt['V']
            W('scalar', 'T', t_ck)
            W('scalar', 'V', v_kpb)
            E_('scalar', 'SC', 1, lambda e: e.activation(eckb[:, :], PSD[:, 256:272], AF.Exp, scale=-1.0))
            sc_eck = cnt['SC']

            W('tensor', 'G', g_state)
            W('tensor', 'V', v_state)
            W('tensor', 'SC', sc_eck)
            W('tensor', 'V', v_kpb)
            for h in range(16):
                hb = h % 2
                po = 64 * (h % 2)
                co = (h // 2) * MC + tok0
                if h % 4 == 0 and h >= 4:
                    W('tensor', 'V')
                if h >= 1:
                    W('tensor', 'SC')
                    W('tensor', 'V')
                for mb in range(2):
                    Eq('tensor', lambda e, _mb=mb, _po=po, _co=co: e.matmul(
                        PSC[:, _mb * 128:(_mb + 1) * 128], projs[_po:_po + 64, _mb * 128:(_mb + 1) * 128],
                        QT[_po:_po + 64, _co:_co + 128], start=True, stop=True))
                for mb in range(2):
                    Eq('tensor', lambda e, _mb=mb, _po=po, _co=co: e.matmul(
                        PSC[:, 256 + _mb * 128:256 + (_mb + 1) * 128], projs[_po:_po + 64, _mb * 128:(_mb + 1) * 128],
                        KT[_po:_po + 64, _co:_co + 128], start=True, stop=True))
                E_('tensor', 'T', 1, lambda e, _po=po, _co=co: e.matmul(
                    PSD[:, 0:256], KT[_po:_po + 64, _co:_co + 128], projs[_po:_po + 64, 0:256],
                    start=True, stop=True))
                t_maps = cnt['T']
                W('scalar', 'T', t_maps)
                W('scalar', 'SC', sc_eck)
                Eq('scalar', lambda e, _hb=hb: e.activation(QPb[_hb][:, :], PSC[:, 0:256], AF.Exp))
                Eq('scalar', lambda e, _hb=hb: e.activation(KPb[_hb][:, :], PSC[:, 256:512], AF.Exp, bias=mln16[:, 0:1]))
                E_('scalar', 'SC', 1, lambda e, _hb=hb, _h=h: e.activation(
                    kptb[_hb][:, :], PSD[:, 0:256], AF.Exp, bias=kpbias[:, _h:_h + 1]))
                sc_exp = cnt['SC']
                W('tensor', 'SC', sc_exp)
                Eq('tensor', lambda e, _hb=hb: e.matmul(
                    PSD[:, 320:448], KPb[_hb][:, 0:128], QPb[_hb][:, 0:128], start=True, stop=False))
                E_('tensor', 'T', 1, lambda e, _hb=hb: e.matmul(
                    PSD[:, 320:448], KPb[_hb][:, 128:256], QPb[_hb][:, 128:256], start=False, stop=True))
                t_A = cnt['T']
                W('vector', 'T', t_A)
                W('vector', 'SC', sc_eck)
                E_('vector', 'V', 1, lambda e, _hb=hb, _h=h: e.scalar_tensor_tensor(
                    Am16[_hb][:, :], PSD[:, 320:448], eckb[:, _h:_h + 1], maskT[:, :],
                    OP.mult, OP.mult))
                v_am = cnt['V']
                W('tensor', 'V', v_am)
                Eq('tensor', lambda e, _hb=hb, _h=h, _tci=tci: e.matmul(
                    PSA[:, _h * 64:(_h + 1) * 64], Am16[_hb][:, :],
                    Vsb[:, _tci * E + _h * 64: _tci * E + (_h + 1) * 64], start=True, stop=False))
                for b in range(2):
                    Eq('tensor', lambda e, _hb=hb, _h=h, _b=b: e.matmul(
                        PSA[:, _h * 64:(_h + 1) * 64], QPb[_hb][:, _b * 128:(_b + 1) * 128],
                        KV16[:, (2 * _h + _b) * 64:(2 * _h + _b + 1) * 64],
                        start=False, stop=(_b == 1)))
                Eq('tensor', lambda e, _hb=hb, _h=h: e.matmul(
                    PSD[:, 272 + _h:272 + _h + 1], Am16[_hb][:, :], ones16[:, 0:1],
                    start=True, stop=False))
                for b in range(2):
                    Eq('tensor', lambda e, _hb=hb, _h=h, _b=b: e.matmul(
                        PSD[:, 272 + _h:272 + _h + 1], QPb[_hb][:, _b * 128:(_b + 1) * 128],
                        ks16[:, 2 * _h + _b:2 * _h + _b + 1], start=False, stop=(_b == 1)))
                for b in range(2):
                    Eq('tensor', lambda e, _hb=hb, _h=h, _b=b, _tci=tci: e.matmul(
                        PSE[:, (_h % 4) * 128 + _b * 64:(_h % 4) * 128 + (_b + 1) * 64],
                        kptb[_hb][:, _b * 128:(_b + 1) * 128],
                        Vsb[:, _tci * E + _h * 64: _tci * E + (_h + 1) * 64],
                        start=True, stop=True))
                for b in range(2):
                    fn = lambda e, _hb=hb, _h=h, _b=b: e.matmul(
                        PSD[:, 288 + (_h % 4) * 2 + _b:288 + (_h % 4) * 2 + _b + 1],
                        kptb[_hb][:, _b * 128:(_b + 1) * 128], ones16[:, 0:1],
                        start=True, stop=True)
                    if b < 1:
                        Eq('tensor', fn)
                    else:
                        E_('tensor', 'T', 1, fn)
                t_head = cnt['T']
                if h % 4 == 3:
                    g0 = h - 3
                    W('vector', 'T', t_head)
                    Eq('vector', lambda e, _g0=g0: e.tensor_tensor(
                        KV32[:, _g0 * 128:(_g0 + 4) * 128], KV32[:, _g0 * 128:(_g0 + 4) * 128],
                        PSE[:, 0:512], OP.add))
                    E_('vector', 'V', 1, lambda e, _g0=g0: e.tensor_tensor(
                        ks32[:, _g0 * 2:(_g0 + 4) * 2], ks32[:, _g0 * 2:(_g0 + 4) * 2],
                        PSD[:, 288:296], OP.add))
                    v_upd = cnt['V']
                    W('gpsimd', 'V', v_upd)
                    Eq('gpsimd', lambda e, _g0=g0: e.tensor_copy(
                        KV16[:, _g0 * 128:(_g0 + 4) * 128], KV32[:, _g0 * 128:(_g0 + 4) * 128]))
                    E_('gpsimd', 'G', 1, lambda e, _g0=g0: e.tensor_copy(
                        ks16[:, _g0 * 2:(_g0 + 4) * 2], ks32[:, _g0 * 2:(_g0 + 4) * 2]))
            g_state = cnt['G']
            v_state = cnt['V']
            if p4_cut == 'heads':
                break

            W('vector', 'T')
            E_('vector', 'V', 1, lambda e: e.reciprocal(recipb[:, :], PSD[:, 272:288]))
            for h in range(16):
                fn = lambda e, _h=h: e.tensor_scalar(
                    atok[:, _h * 64:(_h + 1) * 64], PSA[:, _h * 64:(_h + 1) * 64],
                    recipb[:, _h:_h + 1], None, OP.mult)
                if h < 15:
                    Eq('vector', fn)
                else:
                    E_('vector', 'V', 1, fn)
            v_div = cnt['V']
            if p4_cut == 'div':
                break
            W('tensor', 'V', v_div)
            if ci >= 1:
                W('tensor', 'SC', sc_out_prev.get(ci - 1, 0))
            for b in range(8):
                fn = lambda e, _b=b: e.matmul(
                    PST[:, _b * 128:(_b + 1) * 128], atok[:, _b * 128:(_b + 1) * 128],
                    idn16[:, :], is_transpose=True, start=True, stop=True)
                if b < 7:
                    Eq('tensor', fn)
                else:
                    E_('tensor', 'T', 1, fn)
            t_att = cnt['T']
            W('scalar', 'T', t_att)
            E_('scalar', 'SC', 1, lambda e: e.activation(atT[:, :], PST[:, 0:1024], AF.Copy))
            sc_att = cnt['SC']
            sc_out_prev[ci] = sc_att
            if p4_cut == 'tr':
                break
            W('tensor', 'SC', sc_att)
            if ci >= 1:
                W('tensor', 'V', v_ab_done.get(ci - 1, 0))
            for hf in range(2):
                for kb in range(8):
                    Eq('tensor', lambda e, _hf=hf, _kb=kb: e.matmul(
                        PSB[:, _hf * 512:(_hf + 1) * 512], atT[:, _kb * 128:(_kb + 1) * 128],
                        wbuf[:, OWO + _kb * 1024 + _hf * 512: OWO + _kb * 1024 + (_hf + 1) * 512],
                        start=(_kb == 0), stop=False))
                Eq('tensor', lambda e, _hf=hf: e.matmul(
                    PSB[:, _hf * 512:(_hf + 1) * 512], ones16[32:33, 0:128],
                    rows16[32:33, _hf * 512:(_hf + 1) * 512], start=False, stop=False))
                for j in range(4):
                    b = hf * 4 + j
                    fn = lambda e, _hf=hf, _j=j, _b=b, _ci=ci: e.matmul(
                        PSB[:, _hf * 512 + _j * 128:_hf * 512 + (_j + 1) * 128],
                        xnT[:, _b * S + _ci * CH: _b * S + (_ci + 1) * CH],
                        idn16[:, :], start=False, stop=(_j == 3))
                    if hf == 1 and j == 3:
                        E_('tensor', 'T', 1, fn)
                    else:
                        Eq('tensor', fn)
            t_proj = cnt['T']
            if p4_cut == 'proj':
                break
            W('scalar', 'T', t_proj)
            if ci >= 1:
                W('scalar', 'V', v_ab_done.get(ci - 1, 0))
            E_('scalar', 'SC', 1, lambda e: e.activation(sq2[:, :], PSB[:, 0:1024], AF.Square))
            sc_sq2 = cnt['SC']
            W('vector', 'T', t_proj)
            W('vector', 'SC', sc_sq2)
            Eq('vector', lambda e: e.reduce_sum(ln2s[:, 0:1], PSB[:, 0:512], axis=mybir.AxisListType.X))
            Eq('vector', lambda e: e.reduce_sum(ln2s[:, 1:2], PSB[:, 512:1024], axis=mybir.AxisListType.X))
            Eq('vector', lambda e: e.tensor_add(ln2s[:, 0:1], ln2s[:, 0:1], ln2s[:, 1:2]))
            W('vector', 'SC', sc_sq2)
            Eq('vector', lambda e: e.reduce_sum(ln2s[:, 2:3], sq2[:, :], axis=mybir.AxisListType.X))
            if p4_cut == 'sq':
                mark('V', 1)
                on('vector', lambda e: e.tensor_copy(ln2s[:, 0:1], ln2s[:, 0:1]).then_inc(SEM['V'], 1))
                break
            Eq('vector', lambda e: e.tensor_single_scalar(ln2s[:, 3:4], ln2s[:, 0:1], 1.0 / E, OP.mult))
            Eq('vector', lambda e: e.tensor_single_scalar(ln2s[:, 4:5], ln2s[:, 2:3], 1.0 / E, OP.mult))
            Eq('vector', lambda e: e.tensor_single_scalar(ln2s[:, 5:6], ln2s[:, 3:4], ln2s[:, 3:4], OP.mult))
            Eq('vector', lambda e: e.tensor_sub(ln2s[:, 4:5], ln2s[:, 4:5], ln2s[:, 5:6]))
            Eq('vector', lambda e: e.tensor_single_scalar(ln2s[:, 4:5], ln2s[:, 4:5], LN2_EPS, OP.add))
            E_('vector', 'V', 1, lambda e: e.reciprocal(ln2s[:, 6:7], ln2s[:, 4:5]))
            v_ln2 = cnt['V']
            W('scalar', 'V', v_ln2)
            E_('scalar', 'SC', 1, lambda e: e.activation(ln2s[:, 7:8], ln2s[:, 6:7], AF.Sqrt))
            sc_r2 = cnt['SC']
            if p4_cut == 'ln2':
                break
            if ci >= 2:
                W('sync', 'V', v_ab_done.get(ci - 2, 0))
            E_('sync', 'xr', 16, lambda e, _ci=ci: e.dma_start(
                out=x2t[_ci % 2][:, :], in_=d_x[_ci * CH:(_ci + 1) * CH, :]))
            E_('sync', 'xr', 16, lambda e, _ci=ci: e.dma_start(
                out=g2t[_ci % 2][:, :], in_=d_g[_ci * CH:(_ci + 1) * CH, :]))
            xr_t = cnt['xr']
            W('vector', 'SC', sc_r2)
            Eq('vector', lambda e, _ci=ci: e.tensor_scalar(
                ab[_ci % 2][:, :], PSB[:, 0:1024], ln2s[:, 3:4], ln2s[:, 7:8],
                OP.subtract, OP.mult))
            if p4_cut == 'a1':
                mark('V', 1)
                on('vector', lambda e: e.tensor_copy(ln2s[:, 0:1], ln2s[:, 0:1]).then_inc(SEM['V'], 1))
                break
            Eq('vector', lambda e, _ci=ci: e.tensor_tensor(ab[_ci % 2][:, :], ab[_ci % 2][:, :], gammab[:, :], OP.mult))
            Eq('vector', lambda e, _ci=ci: e.tensor_tensor(ab[_ci % 2][:, :], ab[_ci % 2][:, :], betab[:, :], OP.add))
            W('vector', 'xr', xr_t)
            Eq('vector', lambda e, _ci=ci: e.tensor_tensor(ab[_ci % 2][:, :], ab[_ci % 2][:, :], g2t[_ci % 2][:, :], OP.mult))
            if p4_cut == 'a4':
                mark('V', 1)
                on('vector', lambda e: e.tensor_copy(ln2s[:, 0:1], ln2s[:, 0:1]).then_inc(SEM['V'], 1))
                break
            W('vector', 'xr', xr_t)
            if ci >= 2:
                W('vector', 'out', 16 * (ci - 1))
            E_('vector', 'V', 1, lambda e, _ci=ci: e.tensor_tensor(
                ab2[_ci % 2][:, :], ab[_ci % 2][:, :], x2t[_ci % 2][:, :], OP.add))
            v_ab = cnt['V']
            v_ab_done[ci] = v_ab
            W('sync', 'V', v_ab)
            E_('sync', 'out', 16, lambda e, _ci=ci: e.dma_start(
                out=d_out[_ci * CH:(_ci + 1) * CH, :], in_=ab2[_ci % 2][:, :]))
        if p4_cut in ('heads', 'div', 'tr', 'proj', 'ln2', 'a1', 'a4', 'sq'):
            break

    if p4_dbg:
        W('vector', 'T', cnt['T'])
        W('vector', 'SC', cnt['SC'])
        Eq('vector', lambda e: e.tensor_copy(st1[:, 0:8], sc[:, 0:8]))
        Eq('vector', lambda e: e.tensor_copy(xb[0][:, 0:12], sc[:, 0:12]))
        E_('vector', 'V', 1, lambda e: e.tensor_copy(xb[0][:, 12:13], cmag[:, 0:1]))
        Eq('vector', lambda e: e.tensor_copy(xb[0][:, 16:32], kpbias[:, 0:16]))
        Eq('vector', lambda e: e.tensor_copy(xb[0][:, 32:48], eckb[:, 0:16]))
        E_('vector', 'V', 1, lambda e: e.tensor_copy(xb[0][:, 48:64], recipb[:, 0:16]))
        W('sync', 'V', cnt['V'])
        E_('sync', 'out', 16, lambda e: e.dma_start(out=d_dbg[:, :], in_=xb[0][:, 0:64]))
    if p4_cut is not None or (p4_mc is not None and p4_mc < NMC):
        W('sync', 'gl', cnt['gl'])
        W('sync', 'T', cnt['T'])
        W('sync', 'V', cnt['V'])
        W('sync', 'SC', cnt['SC'])
        W('sync', 'G', cnt['G'])
    W('sync', 'out', cnt['out'])

    if audit_sink is not None:
        audit_sink['steps'] = steps
        audit_sink['SEM'] = SEM
        es.close()
        return nc
    with es, nc.Block() as block:
        for eng in ("sync", "gpsimd", "vector", "scalar", "tensor"):
            def body(e, _eng=eng):
                for (tag, fn) in steps:
                    if tag == _eng:
                        fn(e)
            getattr(block, eng)(body)

    return nc



def _vap_info(ap):
    t = ap.tensor
    tn = type(t).__name__
    if "SB" not in tn and "PSum" not in tn:
        return None
    dims = [tuple(x) for x in ap.ap]
    pstep, pcnt = dims[0]
    try:
        plo = ap.base_partition()
    except Exception:
        plo = 0
    coloff = ap.offset - plo * pstep if pstep else ap.offset
    span = 0
    for (s, c) in dims[1:]:
        span += s * (c - 1)
    return (t.name, plo, plo + pcnt, coloff, coloff + span + 1)


class _VRec:

    def __init__(self, sink):
        self.sink = sink

    def _aps(self, xs):
        out = []
        for x in xs:
            if hasattr(x, "tensor"):
                i = _vap_info(x)
                if i:
                    out.append(i)
        return out

    def _rec(self, reads, writes):
        self.sink.append((self._aps(reads), self._aps(writes)))
        return self

    def then_inc(self, sem, by=1):
        return self

    def wait_ge(self, sem, v):
        return self

    def reciprocal(self, out, in_):
        return self._rec([in_], [out])

    def tensor_scalar(self, out, in0, s1, s2, op0, op1=None, **kw):
        return self._rec([in0, s1, s2], [out])

    def tensor_single_scalar(self, out, in_, s, op, **kw):
        return self._rec([in_, s], [out])

    def scalar_tensor_tensor(self, out, in0, s, in1, op0, op1, **kw):
        return self._rec([in0, s, in1], [out])

    def tensor_tensor(self, out, a, b, op, **kw):
        return self._rec([a, b], [out])

    def tensor_add(self, out, a, b, **kw):
        return self._rec([a, b], [out])

    def tensor_sub(self, out, a, b, **kw):
        return self._rec([a, b], [out])

    def tensor_max(self, out, a, b, **kw):
        return self._rec([a, b], [out])

    def tensor_copy(self, out, in_, **kw):
        return self._rec([in_], [out])

    def reduce_max(self, out, in_, **kw):
        return self._rec([in_], [out])

    def reduce_sum(self, out, in_, **kw):
        return self._rec([in_], [out])

    def memset(self, ap, c, **kw):
        return self._rec([], [ap])


def _overlap(a, b):
    return (a[0] == b[0] and not (a[4] <= b[3] or b[4] <= a[3]
                                  or a[2] <= b[1] or b[2] <= a[1]))


def compute_vraw(S, ncores, **kw):
    sink = {}
    build_program(S, ncores, audit_sink=sink, **kw)
    recs = []
    rec = _VRec(recs)
    for (eng, fn) in sink["steps"]:
        if eng == "vector":
            fn(rec)
    waits = {}
    writes = []
    guaranteed = 0
    for o, (rds, wrs) in enumerate(recs):
        need = 0
        for r in rds:
            for (w, vc) in writes:
                if vc > need and _overlap(r, w):
                    need = vc
        if need > guaranteed:
            waits[o] = need
            guaranteed = need
        for w in wrs:
            writes.append((w, o + 1))
    return waits


def make_consts():
    idn = np.eye(128).astype(np.float16)
    idnf = np.eye(128).astype(np.float32)
    mT = np.triu(np.ones((128, 128))).astype(np.float16)
    ones16 = np.ones((128, 128), np.float16)
    onesf = np.ones((128, 128), np.float32)
    c2d = np.full((128, 1), 1.0 / (2 * DH), np.float16)
    three = np.full((128, 1), 3.0, np.float32)
    return idn, idnf, mT, ones16, onesf, c2d, three


def kernel(L0_attn_x, L0_attn_q_w, L0_attn_q_b, L0_attn_k_w, L0_attn_k_b,
           L0_attn_v_w, L0_attn_v_b, L0_attn_out_w, L0_attn_out_b,
           L0_attn_proj, L0_attn_gamma, L0_attn_beta,
           L0_ffn_W_in, L0_ffn_b_in, L0_ffn_W_out, L0_ffn_b_out):
    X = np.ascontiguousarray(np.asarray(L0_attn_x, np.float32))
    B, S, _ = X.shape
    assert B == NCORES
    idn, idnf, mT, ones16, onesf, c2d, three = make_consts()
    shared = {
        "q_w": np.ascontiguousarray(np.asarray(L0_attn_q_w, np.float32)),
        "k_w": np.ascontiguousarray(np.asarray(L0_attn_k_w, np.float32)),
        "v_w": np.ascontiguousarray(np.asarray(L0_attn_v_w, np.float32)),
        "out_w": np.ascontiguousarray(np.asarray(L0_attn_out_w, np.float32)),
        "q_b": np.asarray(L0_attn_q_b, np.float32).reshape(1, E),
        "k_b": np.asarray(L0_attn_k_b, np.float32).reshape(1, E),
        "v_b": np.asarray(L0_attn_v_b, np.float32).reshape(1, E),
        "out_b": np.asarray(L0_attn_out_b, np.float32).reshape(1, E),
        "projT": np.ascontiguousarray(np.asarray(L0_attn_proj, np.float32).T),
        "gamma": np.asarray(L0_attn_gamma, np.float32).reshape(1, E),
        "beta": np.asarray(L0_attn_beta, np.float32).reshape(1, E),
        "W_in": np.ascontiguousarray(np.asarray(L0_ffn_W_in, np.float32)),
        "b_in": np.asarray(L0_ffn_b_in, np.float32).reshape(1, FF),
        "W_out": np.ascontiguousarray(np.asarray(L0_ffn_W_out, np.float32)),
        "b_out": np.asarray(L0_ffn_b_out, np.float32).reshape(1, E),
        "idn16": idn, "idnf": idnf, "maskT": mT, "ones16": ones16,
        "onesf": onesf, "c2d": c2d, "three": three,
        "mln16": np.full((128, 1), -LN16, np.float32),
    }
    in_maps = [dict(shared, x=np.ascontiguousarray(X[c])) for c in range(NCORES)]
    vw = compute_vraw(S, NCORES)
    nc = build_program(S, NCORES, vraw_waits=vw)
    res = run_bass_kernel_spmd(nc, in_maps, core_ids=list(range(NCORES)))
    out = np.stack([np.asarray(res.results[c]["out"], np.float32) for c in range(NCORES)])
    return out


# version 06 (final)
# speedup vs baseline: 1.4673x; 1.1744x over previous
import math
from collections import defaultdict
from contextlib import ExitStack

import numpy as np
import ml_dtypes

import concourse.bass as bass
import concourse.mybir as mybir
from concourse.bass_utils import run_bass_kernel_spmd

F32 = mybir.dt.float32
F16 = mybir.dt.float16
AF = mybir.ActivationFunctionType
OP = mybir.AluOpType

NCORES = 8
E = 1024
FF = 4096
H = 16
DH = 64
M = 256
CH = 128
LN16 = float(np.float32(np.log(np.float32(16.0))))
LN1_EPS = 1e-5
LN2_EPS = 1e-6


P4_SER = 0


def build_program(S, ncores, last_phase=4, p4_mc=None, p4_cut=None, audit_sink=None, p4_dbg=False, vraw_waits=None):
    NCH = S // CH
    MC = 256
    NMC = S // MC
    TT = 512
    NTT = S // TT
    TCG = 256
    NTCG = S // TCG

    nc = bass.Bass("TRN2", target_bir_lowering=False, debug=False,
                   detect_race_conditions=False)

    d_x = nc.dram_tensor("x", [S, E], F32, kind="ExternalInput")
    d_qw = nc.dram_tensor("q_w", [E, E], F32, kind="ExternalInput")
    d_kw = nc.dram_tensor("k_w", [E, E], F32, kind="ExternalInput")
    d_vw = nc.dram_tensor("v_w", [E, E], F32, kind="ExternalInput")
    d_ow = nc.dram_tensor("out_w", [E, E], F32, kind="ExternalInput")
    d_qb = nc.dram_tensor("q_b", [1, E], F32, kind="ExternalInput")
    d_kb = nc.dram_tensor("k_b", [1, E], F32, kind="ExternalInput")
    d_vb = nc.dram_tensor("v_b", [1, E], F32, kind="ExternalInput")
    d_ob = nc.dram_tensor("out_b", [1, E], F32, kind="ExternalInput")
    d_projT = nc.dram_tensor("projT", [DH, M], F32, kind="ExternalInput")
    d_gamma = nc.dram_tensor("gamma", [1, E], F32, kind="ExternalInput")
    d_beta = nc.dram_tensor("beta", [1, E], F32, kind="ExternalInput")
    d_win = nc.dram_tensor("W_in", [E, FF], F32, kind="ExternalInput")
    d_bin = nc.dram_tensor("b_in", [1, FF], F32, kind="ExternalInput")
    d_wout = nc.dram_tensor("W_out", [FF, E], F32, kind="ExternalInput")
    d_bout = nc.dram_tensor("b_out", [1, E], F32, kind="ExternalInput")
    d_idn16 = nc.dram_tensor("idn16", [128, 128], F16, kind="ExternalInput")
    d_idnf = nc.dram_tensor("idnf", [128, 128], F32, kind="ExternalInput")
    d_maskT = nc.dram_tensor("maskT", [128, 128], F16, kind="ExternalInput")
    d_ones16 = nc.dram_tensor("ones16", [128, 128], F16, kind="ExternalInput")
    d_onesf = nc.dram_tensor("onesf", [128, 128], F32, kind="ExternalInput")
    d_c2d = nc.dram_tensor("c2d", [128, 1], F16, kind="ExternalInput")
    d_three = nc.dram_tensor("three", [128, 1], F32, kind="ExternalInput")
    d_mln16 = nc.dram_tensor("mln16", [128, 1], F32, kind="ExternalInput")
    d_out = nc.dram_tensor("out", [S, E], F32, kind="ExternalOutput")

    dbg_kind = "ExternalOutput" if p4_dbg else "Internal"
    d_hid = nc.dram_tensor("hid_scr", [FF, S], F16, kind=dbg_kind) if p4_dbg else nc.dram_tensor("hid_scr", [FF, S], F16)
    d_g = nc.dram_tensor("g_scr", [S, E], F16, kind=dbg_kind) if p4_dbg else nc.dram_tensor("g_scr", [S, E], F16)
    d_dbg = nc.dram_tensor("dbg", [128, 64], F32, kind="ExternalOutput") if p4_dbg else None
    d_cc1i = nc.dram_tensor("cc1i", [1, 128], F32)
    d_cc1o = nc.dram_tensor("cc1o", [1, 128], F32)
    d_cc2i = nc.dram_tensor("cc2i", [1, 128], F32)
    d_cc2o = nc.dram_tensor("cc2o", [1, 128], F32)

    es = ExitStack()

    def sb(name, shape, dt):
        return es.enter_context(nc.sbuf_tensor(name, shape, dt))

    class View:
        def __init__(self, pool, base, ncols):
            self.pool, self.base, self.ncols = pool, base, ncols

        def __getitem__(self, idx):
            rows, cols = idx
            if isinstance(cols, slice):
                start = 0 if cols.start is None else cols.start
                stop = self.ncols if cols.stop is None else cols.stop
                return self.pool[rows, self.base + start:self.base + stop]
            raise TypeError(cols)

    def psum(name, shape, dt):
        return es.enter_context(nc.psum_tensor(name, shape, dt))

    xnT = sb("xnT", [128, 8 * S], F16)
    wbuf = sb("wbuf", [128, 32768], F16)
    KV32 = sb("KV32", [128, 2048], F32)
    KV16 = sb("KV16", [128, 2048], F16)
    ks32 = sb("ks32", [128, 32], F32)
    ks16 = sb("ks16", [128, 32], F16)

    idn16 = sb("idn16s", [128, 128], F16)
    idnf = sb("idnfs", [128, 128], F32)
    maskT = sb("maskTs", [128, 128], F16)
    ones16 = sb("ones16s", [128, 128], F16)
    onesf = sb("onesfs", [128, 128], F32)
    c2d = sb("c2ds", [128, 1], F16)
    cmag = sb("cmag", [128, 1], F32)
    mln16 = sb("mln16s", [128, 1], F32)
    projs = sb("projs", [128, 256], F16)
    qbT = sb("qbT", [128, 8], F32)
    kbT = sb("kbT", [128, 8], F32)
    binT = sb("binT", [128, 32], F32)
    rows16 = sb("rows16", [128, E], F16)

    amaxr = sb("amaxr", [128, 1], F32)
    hmaxr = sb("hmaxr", [128, 1], F32)
    sc = sb("scl", [128, 12], F32)
    amaxf = sb("amaxf", [128, 32], F32)
    sw1 = sb("sw1", [128, 32], F32)
    invwx = sb("invwx", [128, 32], F32)

    S16 = sb("S16pool", [128, 16128], F16)
    S32 = sb("S32pool", [128, 4096], F32)
    xb = [sb("xb%d" % i, [128, E], F32) for i in range(2)]
    x2t = xb
    st1t = sb("st1", [128, 16], F32)
    mub = [sb("mub%d" % i, [128, 1], F32) for i in range(2)]
    rb = [sb("rb%d" % i, [128, 1], F32) for i in range(2)]
    xntok = [View(S16, 0, E), View(S16, E, E)]
    sqb = View(S16, 2048, E)
    winb = [View(S16, 0, FF), View(S16, FF, FF)]
    wmax = View(S16, 2 * FF, FF)
    scb = View(S16, 2 * FF, FF)
    xsim = [View(S16, 0, 8 * TT), View(S16, 4096, 8 * TT)]
    sty = View(S32, 0, FF)
    stq = View(S32, 0, TT)
    hstg = [View(S16, 8192 + i * TT, TT) for i in range(4)]
    hred = sb("hred", [128, 2], F32)
    wob = [View(S32, 1024, E), View(S32, 2048, E)]
    wos = sb("wos", [128, 8], F32)
    sty2 = View(S32, 0, E)
    bor32v = View(S32, 3072, E)
    gstg = [View(S16, 0, E), View(S16, E, E)]
    hrl = [View(S16, 3072 + i * 512, 512) for i in range(4)]
    ihq = [View(S16, 5120 + i * 512, 512) for i in range(2)]
    stq3 = View(S32, 0, 512)
    QT = View(S16, 0, 8 * MC)
    KT = View(S16, 8 * MC, 8 * MC)
    ksq = View(S16, 16 * MC, 8 * MC)
    Vsb = View(S16, 24 * MC, 2 * E)
    atok = View(S16, 8192, E)
    atT = View(S16, 9216, E)
    sq2 = View(S16, 8192, E)
    g2t = [View(S16, 10240, E), View(S16, 11264, E)]
    QPb = [View(S16, 12288, 256), View(S16, 12544, 256)]
    KPb = [View(S16, 12800, 256), View(S16, 13056, 256)]
    kptb = [View(S16, 13312, 256), View(S16, 13568, 256)]
    Am16 = [View(S16, 13824, 128), View(S16, 13952, 128)]
    kpbias = sb("kpbias", [128, 16], F32)
    eckb = sb("eckb", [128, 16], F32)
    recipb = sb("recipb", [128, 16], F32)
    ab = [View(S32, 0, E), View(S32, 1024, E)]
    ab2 = ab
    ln2s = sb("ln2s", [128, 8], F32)
    stok_sb = sb("stok", [128, E], F32)
    gammab = View(S16, 14080, E)
    betab = View(S16, 15104, E)
    ccst = View(S32, 3072, 128)
    rhsD = View(S32, 1024, 128)
    ccst2 = View(S32, 3200, 128)
    projf = View(S32, 3328, 256)

    PSA = psum("PSA", [128, 1024], F32)
    PSB = psum("PSB", [128, 1024], F32)
    PSC = psum("PSC", [128, 512], F32)
    PSD = psum("PSD", [128, 512], F32)
    PSE = psum("PSE", [128, 512], F32)
    PST = psum("PST", [128, 1024], F16)

    sem_names = ["xl", "xm", "wl", "gw", "ga", "gb", "gc",
                 "h0", "h1", "h2", "h3", "r0", "r1", "r2", "r3",
                 "g0", "g1", "ccd", "cc", "xr", "x0", "x1", "gl",
                 "out", "T", "V", "SC", "G"]
    SEM = {n: es.enter_context(nc.semaphore("s_" + n)) for n in sem_names}

    steps = []
    cnt = defaultdict(int)
    vord = [0]

    def on(eng, fn):
        steps.append((eng, fn))

    def W(eng, s, v=None):
        t = cnt[s] if v is None else v
        assert t <= cnt[s], (s, t, cnt[s])
        if t > 0:
            on(eng, lambda e, _t=t, _s=s: e.wait_ge(SEM[_s], _t))

    def mark(s, by=1):
        cnt[s] += by
        return cnt[s]

    def E_(eng, s, by, fn):
        if eng == 'vector':
            o = vord[0]
            vord[0] += 1
            wv = vraw_waits.get(o) if vraw_waits is not None else None
            if wv:
                on('vector', lambda e, _t=wv: e.wait_ge(SEM['V'], _t))
            if s == 'V' and by == 1:
                mark('V', 1)
                on(eng, lambda e, _fn=fn: _fn(e).then_inc(SEM['V'], 1))
            else:
                raise AssertionError("vector ops must inc V")
            return
        mark(s, by)
        on(eng, lambda e, _fn=fn, _s=s, _by=by: _fn(e).then_inc(SEM[_s], _by))

    def Eq(eng, fn):
        if eng == 'vector':
            E_('vector', 'V', 1, fn)
            return
        on(eng, lambda e, _fn=fn: _fn(e))

    for (dst, src) in [(idn16, d_idn16), (idnf, d_idnf), (maskT, d_maskT),
                       (ones16, d_ones16), (onesf, d_onesf), (c2d, d_c2d),
                       (cmag, d_three), (mln16, d_mln16)]:
        E_('sync', 'wl', 16, lambda e, _d=dst, _s=src: e.dma_start(out=_d[:, :], in_=_s[:, :]))
    E_('sync', 'wl', 16, lambda e: e.dma_start(out=projf[0:64, :], in_=d_projT[:, :]))
    E_('sync', 'wl', 16, lambda e: e.dma_start(out=projf[64:128, :], in_=d_projT[:, :]))
    E_('sync', 'wl', 16, lambda e: e.dma_start(
        out=qbT[:, :], in_=bass.AP(d_qb, 0, [[1, 128], [128, 8]]),
        allow_slow_non_contiguous=True))
    E_('sync', 'wl', 16, lambda e: e.dma_start(
        out=kbT[:, :], in_=bass.AP(d_kb, 0, [[1, 128], [128, 8]]),
        allow_slow_non_contiguous=True))
    E_('sync', 'wl', 16, lambda e: e.dma_start(
        out=binT[:, :], in_=bass.AP(d_bin, 0, [[1, 128], [128, 32]]),
        allow_slow_non_contiguous=True))
    E_('gpsimd', 'gw', 16, lambda e: e.dma_start(out=rows16[0:1, :], in_=d_gamma[:, :]))
    E_('gpsimd', 'gw', 16, lambda e: e.dma_start(out=rows16[32:33, :], in_=d_beta[:, :]))

    wl_p0 = cnt['wl']
    gw_p0 = cnt['gw']

    W('vector', 'wl', wl_p0)
    Eq('vector', lambda e: e.reciprocal(cmag[:, :], cmag[:, :]))
    Eq('vector', lambda e: e.tensor_scalar(cmag[:, :], cmag[:, :], 4.0, 1.0, OP.mult, OP.subtract))
    Eq('vector', lambda e: e.tensor_scalar(cmag[:, :], cmag[:, :], 3.0, 1.0, OP.mult, OP.subtract))
    Eq('vector', lambda e: e.reciprocal(cmag[:, :], cmag[:, :]))
    E_('vector', 'V', 1, lambda e: e.tensor_single_scalar(cmag[:, :], cmag[:, :], 1.5, OP.mult))
    v_cmag = cnt['V']

    W('scalar', 'wl', wl_p0)
    E_('scalar', 'SC', 1, lambda e: e.activation(projs[:, :], projf[:, :], AF.Copy,
                                                 scale=1.0 / math.sqrt(DH)))
    W('tensor', 'wl', wl_p0)
    W('tensor', 'gw', gw_p0)
    E_('tensor', 'T', 1, lambda e: e.matmul(PSC[:, 0:512], ones16[0:1, 0:128], rows16[0:1, 0:512], start=True, stop=True))
    E_('tensor', 'T', 1, lambda e: e.matmul(PSD[:, 0:512], ones16[0:1, 0:128], rows16[0:1, 512:1024], start=True, stop=True))
    t_g1 = cnt['T']
    W('scalar', 'T', t_g1)
    E_('scalar', 'SC', 1, lambda e: e.activation(gammab[:, 0:512], PSC[:, 0:512], AF.Copy))
    E_('scalar', 'SC', 1, lambda e: e.activation(gammab[:, 512:1024], PSD[:, 0:512], AF.Copy))
    sc_g1 = cnt['SC']
    W('tensor', 'SC', sc_g1)
    E_('tensor', 'T', 1, lambda e: e.matmul(PSC[:, 0:512], ones16[32:33, 0:128], rows16[32:33, 0:512], start=True, stop=True))
    E_('tensor', 'T', 1, lambda e: e.matmul(PSD[:, 0:512], ones16[32:33, 0:128], rows16[32:33, 512:1024], start=True, stop=True))
    t_g2 = cnt['T']
    W('scalar', 'T', t_g2)
    E_('scalar', 'SC', 1, lambda e: e.activation(betab[:, 0:512], PSC[:, 0:512], AF.Copy))
    E_('scalar', 'SC', 1, lambda e: e.activation(betab[:, 512:1024], PSD[:, 0:512], AF.Copy))
    sc_g2 = cnt['SC']
    W('gpsimd', 'T', t_g2)
    E_('gpsimd', 'gw', 16, lambda e: e.dma_start(out=rows16[0:1, :], in_=d_vb[:, :]))
    E_('gpsimd', 'gw', 16, lambda e: e.dma_start(out=rows16[32:33, :], in_=d_ob[:, :]))

    v_xn = {}
    sc_ev = {}

    class _Off:
        def __init__(self, t, off):
            self.t, self.off = t, off

        def __getitem__(self, idx):
            rows, cols = idx
            return self.t[rows, cols.start + self.off:cols.stop + self.off]
    for ci in range(NCH):
        _o = 8 * (ci % 2)
        st1 = _Off(st1t, _o)
        if ci >= 2:
            W('sync', 'V', v_xn[ci - 2])
        _xs = 'xl' if ci % 2 == 0 else 'xm'
        E_('sync', _xs, 16, lambda e, _ci=ci: e.dma_start(
            out=xb[_ci % 2][:, :], in_=d_x[_ci * CH:(_ci + 1) * CH, :]))
        W('scalar', _xs, 16 * (ci // 2 + 1))
        if ci >= 1:
            W('scalar', 'V', v_xn[ci - 1])
        E_('scalar', 'SC', 1, lambda e, _ci=ci: e.activation(sqb[:, :], xb[_ci % 2][:, :], AF.Square))
        sc_sq = cnt['SC']
        W('vector', _xs, 16 * (ci // 2 + 1))
        _c0, _c1, _c2, _c3, _c4, _c5, _c6 = (st1t[:, _o:_o+1], st1t[:, _o+1:_o+2], st1t[:, _o+2:_o+3],
                                             st1t[:, _o+3:_o+4], st1t[:, _o+4:_o+5], st1t[:, _o+5:_o+6],
                                             st1t[:, _o+6:_o+7])
        Eq('vector', lambda e, _ci=ci, _a=_c0: e.reduce_sum(_a, xb[_ci % 2][:, :], axis=mybir.AxisListType.X))
        W('vector', 'SC', sc_sq)
        Eq('vector', lambda e, _a=_c1: e.reduce_sum(_a, sqb[:, :], axis=mybir.AxisListType.X))
        Eq('vector', lambda e, _ci=ci, _a=_c0: e.tensor_single_scalar(mub[_ci % 2][:, :], _a, 1.0 / E, OP.mult))
        Eq('vector', lambda e, _a=_c2, _b=_c1: e.tensor_single_scalar(_a, _b, 1.0 / E, OP.mult))
        Eq('vector', lambda e, _ci=ci, _a=_c3: e.tensor_single_scalar(_a, mub[_ci % 2][:, :], mub[_ci % 2][:, 0:1], OP.mult))
        Eq('vector', lambda e, _a=_c4, _b=_c2, _c=_c3: e.tensor_sub(_a, _b, _c))
        Eq('vector', lambda e, _a=_c4: e.tensor_single_scalar(_a, _a, LN1_EPS, OP.add))
        E_('vector', 'V', 1, lambda e, _a=_c5, _b=_c4: e.reciprocal(_a, _b))
        v_vr = cnt['V']
        W('scalar', 'V', v_vr)
        E_('scalar', 'SC', 1, lambda e, _ci=ci, _b=_c5: e.activation(rb[_ci % 2][:, :], _b, AF.Sqrt))
        sc_r = cnt['SC']
        W('vector', 'SC', sc_r)
        Eq('vector', lambda e, _ci=ci: e.tensor_scalar(
            xntok[_ci % 2][:, :], xb[_ci % 2][:, :], mub[_ci % 2][:, 0:1], rb[_ci % 2][:, 0:1],
            OP.subtract, OP.mult))
        Eq('vector', lambda e, _ci=ci, _a=_c6: e.reduce_max(_a, xntok[_ci % 2][:, :],
                                                    axis=mybir.AxisListType.X, apply_absolute_value=True))
        if ci == 0:
            E_('vector', 'V', 1, lambda e, _a=_c6: e.tensor_copy(amaxr[:, :], _a))
        else:
            E_('vector', 'V', 1, lambda e, _a=_c6: e.tensor_max(amaxr[:, :], amaxr[:, :], _a))
        v_xn[ci] = cnt['V']
        W('tensor', 'V', v_xn[ci])
        if ci >= 1:
            W('tensor', 'SC', sc_ev[ci - 1])
        for b in range(8):
            if b < 7:
                Eq('tensor', lambda e, _ci=ci, _b=b: e.matmul(
                    PST[:, _b * 128:(_b + 1) * 128], xntok[_ci % 2][:, _b * 128:(_b + 1) * 128],
                    idn16[:, :], is_transpose=True, start=True, stop=True))
            else:
                E_('tensor', 'T', 1, lambda e, _ci=ci, _b=b: e.matmul(
                    PST[:, _b * 128:(_b + 1) * 128], xntok[_ci % 2][:, _b * 128:(_b + 1) * 128],
                    idn16[:, :], is_transpose=True, start=True, stop=True))
        t_tr = cnt['T']
        W('scalar', 'T', t_tr)
        for b in range(8):
            if b < 7:
                Eq('scalar', lambda e, _ci=ci, _b=b: e.activation(
                    xnT[:, _b * S + _ci * CH: _b * S + (_ci + 1) * CH],
                    PST[:, _b * 128:(_b + 1) * 128], AF.Copy))
            else:
                E_('scalar', 'SC', 1, lambda e, _ci=ci, _b=b: e.activation(
                    xnT[:, _b * S + _ci * CH: _b * S + (_ci + 1) * CH],
                    PST[:, _b * 128:(_b + 1) * 128], AF.Copy))
        sc_ev[ci] = cnt['SC']

    st1 = _Off(st1t, 0)
    def amax_collective(src_run, cc_in, cc_out, col_amax, col_s, col_inv, qscale):
        W('tensor', 'V')
        E_('tensor', 'T', 1, lambda e: e.matmul(PSD[0:1, 0:128], src_run[:, 0:1], idnf[:, :],
                                                is_transpose=True, start=True, stop=True))
        W('vector', 'T')
        E_('vector', 'V', 1, lambda e: e.tensor_copy(ccst[0:1, 0:128], PSD[0:1, 0:128]))
        W('gpsimd', 'V')
        E_('gpsimd', 'ccd', 16, lambda e: e.dma_start(out=cc_in[0:1, 0:128], in_=ccst[0:1, 0:128]))
        W('gpsimd', 'ccd')
        E_('gpsimd', 'cc', 1, lambda e: e.collective_compute(
            "AllReduce", OP.max, replica_groups=[list(range(ncores))],
            ins=[cc_in.ap().opt()], outs=[cc_out.ap().opt()]))
        W('gpsimd', 'cc')
        E_('gpsimd', 'ccd', 16, lambda e: e.dma_start(out=ccst2[0:1, 0:128], in_=cc_out[0:1, 0:128]))
        W('vector', 'ccd')
        W('vector', 'cc')
        Eq('vector', lambda e: e.reduce_max(ccst[0:1, 0:1], ccst2[0:1, 0:128], axis=mybir.AxisListType.X))
        E_('vector', 'V', 1, lambda e: e.tensor_single_scalar(ccst[0:1, 0:1], ccst[0:1, 0:1], 1e-8, OP.max))
        W('tensor', 'V')
        E_('tensor', 'T', 1, lambda e: e.matmul(PSD[0:128, 128:129], onesf[0:1, 0:128],
                                                ccst[0:1, 0:1], start=True, stop=True))
        W('vector', 'T')
        Eq('vector', lambda e: e.tensor_copy(sc[:, col_amax:col_amax + 1], PSD[:, 128:129]))
        Eq('vector', lambda e: e.reciprocal(sc[:, 6:7], sc[:, col_amax:col_amax + 1]))
        Eq('vector', lambda e, _q=qscale: e.tensor_single_scalar(sc[:, col_s:col_s + 1], sc[:, 6:7], _q, OP.mult))
        E_('vector', 'V', 1, lambda e: e.reciprocal(sc[:, col_inv:col_inv + 1], sc[:, col_s:col_s + 1]))

    amax_collective(amaxr, d_cc1i, d_cc1o, 0, 1, 2, 32767.0)
    v_scales_x = cnt['V']

    for eb in range(8):
        if eb == 0:
            W('gpsimd', 'V', cnt['V'])
        else:
            W('gpsimd', 'SC')
        E_('gpsimd', 'ga', 16, lambda e, _eb=eb: e.dma_start(
            out=winb[0][:, :], in_=d_win[_eb * 128:(_eb + 1) * 128, :]))
        W('scalar', 'ga', cnt['ga'])
        if eb >= 2:
            W('scalar', 'V')
        E_('scalar', 'SC', 1, lambda e: e.activation(winb[1][:, :], winb[0][:, :], AF.Abs))
        W('vector', 'SC', cnt['SC'])
        if eb == 0:
            E_('vector', 'V', 1, lambda e: e.tensor_copy(wmax[:, :], winb[1][:, :]))
        else:
            E_('vector', 'V', 1, lambda e: e.tensor_tensor(
                wmax[:, :], wmax[:, :], winb[1][:, :], OP.max))
    v_wmax = cnt['V']
    for rnd in range(4):
        W('tensor', 'V', v_wmax)
        if rnd == 0:
            W('tensor', 'SC', sc_ev[NCH - 1])
        if rnd >= 1:
            W('tensor', 'V')
        for j in range(8):
            fb = rnd * 8 + j
            fn = lambda e, _fb=fb, _j=j: e.matmul(
                PST[:, _j * 128:(_j + 1) * 128], wmax[:, _fb * 128:(_fb + 1) * 128],
                idn16[:, :], is_transpose=True, start=True, stop=True)
            if j < 7:
                Eq('tensor', fn)
            else:
                E_('tensor', 'T', 1, fn)
        W('vector', 'T', cnt['T'])
        for j in range(8):
            fb = rnd * 8 + j
            fn = lambda e, _fb=fb, _j=j: e.reduce_max(
                amaxf[:, _fb:_fb + 1], PST[:, _j * 128:(_j + 1) * 128], axis=mybir.AxisListType.X)
            if j < 7:
                Eq('vector', fn)
            else:
                E_('vector', 'V', 1, fn)
    W('vector', 'V')
    Eq('vector', lambda e: e.tensor_single_scalar(amaxf[:, :], amaxf[:, :], 1e-8, OP.max))
    Eq('vector', lambda e: e.reciprocal(sw1[:, :], amaxf[:, :]))
    Eq('vector', lambda e: e.tensor_single_scalar(sw1[:, :], sw1[:, :], 127.0, OP.mult))
    Eq('vector', lambda e: e.reciprocal(invwx[:, :], sw1[:, :]))
    W('vector', 'V', v_scales_x)
    E_('vector', 'V', 1, lambda e: e.tensor_single_scalar(invwx[:, :], invwx[:, :], sc[:, 2:3], OP.mult))
    v_sw1 = cnt['V']
    for grp in range(8):
        for j in range(4):
            b = grp * 4 + j
            fn = lambda e, _b=b: e.tensor_scalar(
                rhsD[:, 0:128], idnf[:, :], sw1[:, _b:_b + 1], None, OP.mult)
            if b == 0:
                W('vector', 'V', v_sw1)
            else:
                W('vector', 'T')
            E_('vector', 'V', 1, fn)
            W('tensor', 'V', cnt['V'])
            if grp >= 1 and j == 0:
                W('tensor', 'SC')
            fn2 = lambda e, _j=j: e.matmul(
                PSD[:, _j * 128:(_j + 1) * 128], onesf[:, 0:128], rhsD[:, 0:128],
                start=True, stop=True)
            E_('tensor', 'T', 1, fn2)
        W('scalar', 'T', cnt['T'])
        fn = lambda e, _g=grp: e.activation(scb[:, _g * 512:(_g + 1) * 512], PSD[:, 0:512], AF.Copy)
        E_('scalar', 'SC', 1, fn)
    sc_scb = cnt['SC']
    W('vector', 'SC', sc_scb)
    W('vector', 'V', v_cmag)
    for eb in range(8):
        if eb >= 2:
            W('gpsimd', 'V', cnt['V'])
        else:
            W('gpsimd', 'V', v_wmax)
        _ws = 'gb' if eb % 2 == 0 else 'gc'
        E_('gpsimd', _ws, 16, lambda e, _eb=eb: e.dma_start(
            out=winb[_eb % 2][:, :], in_=d_win[_eb * 128:(_eb + 1) * 128, :]))
        W('vector', _ws, cnt[_ws])
        Eq('vector', lambda e, _eb=eb: e.tensor_tensor(sty[:, :], winb[_eb % 2][:, :], scb[:, :], OP.mult))
        E_('vector', 'V', 1, lambda e, _eb=eb: e.tensor_scalar(
            wbuf[:, _eb * FF:(_eb + 1) * FF], sty[:, :], cmag[:, 0:1], cmag[:, 0:1], OP.add, OP.subtract))
    v_win_ready = cnt['V']

    sc_gelu = {}
    g_hdone = {}
    t_ffn1 = {}
    for tt in range(NTT):
        W('vector', 'V', v_scales_x)
        if tt >= 2:
            W('tensor', 'V')
        if tt >= 2:
            W('vector', 'T', t_ffn1[tt - 2])
        for eb in range(8):
            Eq('vector', lambda e, _tt=tt, _eb=eb: e.tensor_scalar(
                stq[:, :], xnT[:, _eb * S + _tt * TT: _eb * S + (_tt + 1) * TT],
                sc[:, 1:2], cmag[:, 0:1], OP.mult, OP.add))
            fn = lambda e, _tt=tt, _eb=eb: e.tensor_scalar(
                xsim[_tt % 2][:, _eb * TT:(_eb + 1) * TT], stq[:, :], cmag[:, 0:1], None, OP.subtract)
            if eb < 7:
                Eq('vector', fn)
            else:
                E_('vector', 'V', 1, fn)
        v_xsim = cnt['V']
        W('tensor', 'V', v_xsim)
        if tt == 0:
            W('tensor', 'V', v_win_ready)
        for fb in range(32):
            ps = PSC if fb % 2 == 0 else PSD
            if fb >= 2:
                W('tensor', 'SC', sc_gelu[(tt, fb - 2)])
            elif tt >= 1:
                W('tensor', 'SC', sc_gelu[(tt - 1, fb + 30)])
            for kb in range(8):
                fn = lambda e, _ps=ps, _fb=fb, _kb=kb, _tt=tt: e.matmul(
                    _ps[:, 0:TT], wbuf[:, _kb * FF + _fb * 128: _kb * FF + (_fb + 1) * 128],
                    xsim[_tt % 2][:, _kb * TT:(_kb + 1) * TT],
                    start=(_kb == 0), stop=(_kb == 7))
                if kb < 7:
                    Eq('tensor', fn)
                else:
                    E_('tensor', 'T', 1, fn)
            t_mm = cnt['T']
            W('scalar', 'T', t_mm)
            if (tt * 32 + fb) >= 4:
                pi = (tt * 32 + fb) - 4
                W('scalar', 'V', g_hdone.get(pi, 0))
                W('scalar', 'h%d' % (pi % 4), 16 * (pi // 4 + 1))
            E_('scalar', 'SC', 1, lambda e, _ps=ps, _fb=fb: e.activation(
                hstg[_fb % 4][:, :], _ps[:, 0:TT], AF.Gelu,
                bias=binT[:, _fb:_fb + 1], scale=invwx[:, _fb:_fb + 1]))
            sc_gelu[(tt, fb)] = cnt['SC']
            W('vector', 'SC', cnt['SC'])
            Eq('vector', lambda e, _fb=fb: e.reduce_max(
                hred[:, 0:1], hstg[_fb % 4][:, :], axis=mybir.AxisListType.X,
                apply_absolute_value=True))
            if tt == 0 and fb == 0:
                E_('vector', 'V', 1, lambda e: e.tensor_copy(hmaxr[:, :], hred[:, 0:1]))
            else:
                E_('vector', 'V', 1, lambda e: e.tensor_max(hmaxr[:, :], hmaxr[:, :], hred[:, 0:1]))
            g_hdone[tt * 32 + fb] = cnt['V']
            W('gpsimd', 'SC', cnt['SC'])
            E_('gpsimd', 'h%d' % ((tt * 32 + fb) % 4), 16, lambda e, _tt=tt, _fb=fb: e.dma_start(
                out=d_hid[_fb * 128:(_fb + 1) * 128, _tt * TT:(_tt + 1) * TT],
                in_=hstg[_fb % 4][:, :]))
        t_ffn1[tt] = cnt['T']

    if last_phase < 3:
        for _s in ('h0', 'h1', 'h2', 'h3'):
            W('sync', _s, cnt[_s])
        if audit_sink is not None:
            audit_sink['steps'] = steps; audit_sink['SEM'] = SEM; es.close(); return nc
        with es, nc.Block() as block:
            for eng in ("sync", "gpsimd", "vector", "scalar", "tensor"):
                def body(e, _eng=eng):
                    for (tag, fn) in steps:
                        if tag == _eng:
                            fn(e)
                getattr(block, eng)(body)
        return nc
    amax_collective(hmaxr, d_cc2i, d_cc2o, 3, 4, 5, 32767.0)
    v_scales_h = cnt['V']
    W('sync', 'V', cnt['V'])
    E_('sync', 'xr', 16, lambda e: e.dma_start(out=bor32v[0:1, :], in_=d_bout[:, :]))
    W('vector', 'xr', cnt['xr'])
    E_('vector', 'V', 1, lambda e: e.tensor_single_scalar(rows16[64:65, :], bor32v[0:1, :], sc[64:65, 4:5], OP.mult))
    v_bor = cnt['V']

    sc_wo_start = cnt['SC']
    W('sync', 'V', cnt['V'])
    for fb in range(32):
        if fb >= 2:
            W('sync', 'V')
        E_('sync', 'xr', 16, lambda e, _fb=fb: e.dma_start(
            out=wob[_fb % 2][:, :], in_=d_wout[_fb * 128:(_fb + 1) * 128, :]))
        W('vector', 'xr', cnt['xr'])
        W('vector', 'T', t_ffn1[NTT - 1])
        Eq('vector', lambda e, _fb=fb: e.reduce_max(
            wos[:, 0:1], wob[_fb % 2][:, :], axis=mybir.AxisListType.X, apply_absolute_value=True))
        Eq('vector', lambda e: e.tensor_single_scalar(wos[:, 0:1], wos[:, 0:1], 1e-8, OP.max))
        Eq('vector', lambda e: e.reciprocal(wos[:, 1:2], wos[:, 0:1]))
        Eq('vector', lambda e: e.tensor_single_scalar(wos[:, 1:2], wos[:, 1:2], 127.0, OP.mult))
        Eq('vector', lambda e: e.reciprocal(wos[:, 2:3], wos[:, 1:2]))
        Eq('vector', lambda e, _fb=fb: e.tensor_scalar(
            sty2[:, :], wob[_fb % 2][:, :], wos[:, 1:2], cmag[:, 0:1], OP.mult, OP.add))
        E_('vector', 'V', 1, lambda e, _fb=fb: e.tensor_scalar(
            wbuf[:, _fb * E:(_fb + 1) * E], sty2[:, :], cmag[:, 0:1], wos[:, 2:3],
            OP.subtract, OP.mult))
    v_wout_ready = cnt['V']

    sc_silu = {}
    t_ffn2 = {}
    g_gst = {}
    W('gpsimd', 'T', t_ffn1[NTT - 1])
    for _s in ('h0', 'h1', 'h2', 'h3'):
        W('gpsimd', _s, cnt[_s])
    v_quant = {}
    for tg in range(NTCG):
        for fbp in range(16):
            idx = tg * 16 + fbp
            if idx >= 4:
                W('gpsimd', 'V', v_quant.get(idx - 4, 0))
            _rs = 'r%d' % (idx % 4)
            E_('gpsimd', _rs, 16, lambda e, _tg=tg, _fbp=fbp: e.dma_start(
                out=hrl[_fbp % 4][:, :],
                in_=bass.AP(d_hid, (_fbp * 256) * S + _tg * TCG,
                            [[S, 128], [128 * S, 2], [1, TCG]])))
            W('vector', _rs, cnt[_rs])
            W('vector', 'V', v_scales_h)
            if idx >= 2:
                W('vector', 'T', t_ffn2.get(idx - 2, 0))
            Eq('vector', lambda e, _fbp=fbp: e.tensor_scalar(
                stq3[:, :], hrl[_fbp % 4][:, :], sc[:, 4:5], cmag[:, 0:1], OP.mult, OP.add))
            E_('vector', 'V', 1, lambda e, _fbp=fbp: e.tensor_scalar(
                ihq[_fbp % 2][:, :], stq3[:, :], cmag[:, 0:1], None, OP.subtract))
            v_quant[idx] = cnt['V']
            W('tensor', 'V', cnt['V'])
            if fbp == 0:
                W('tensor', 'V', v_wout_ready)
                if tg >= 1:
                    W('tensor', 'SC', sc_silu[(tg - 1, 1)])
            for j in range(2):
                fb = 2 * fbp + j
                for tc in range(2):
                    PS = PSA if tc == 0 else PSB
                    for hf in range(2):
                        fn = lambda e, _PS=PS, _fb=fb, _j=j, _tc=tc, _hf=hf, _fbp=fbp: e.matmul(
                            _PS[:, _hf * 512:(_hf + 1) * 512],
                            ihq[_fbp % 2][:, _j * 256 + _tc * 128: _j * 256 + (_tc + 1) * 128],
                            wbuf[:, _fb * E + _hf * 512: _fb * E + (_hf + 1) * 512],
                            start=(_fb == 0), stop=False)
                        if j == 1 and tc == 1 and hf == 1:
                            E_('tensor', 'T', 1, fn)
                        else:
                            Eq('tensor', fn)
            t_ffn2[idx] = cnt['T']
        W('tensor', 'V', v_bor)
        for tc in range(2):
            PS = PSA if tc == 0 else PSB
            for hf in range(2):
                fn = lambda e, _PS=PS, _tc=tc, _hf=hf: e.matmul(
                    _PS[:, _hf * 512:(_hf + 1) * 512], ones16[64:65, 0:128],
                    rows16[64:65, _hf * 512:(_hf + 1) * 512], start=False, stop=True)
                if hf == 1:
                    E_('tensor', 'T', 1, fn)
                else:
                    Eq('tensor', fn)
            t_bias = cnt['T']
            W('scalar', 'T', t_bias)
            if tg >= 1:
                W('scalar', 'g%d' % tc, 16 * tg)
            E_('scalar', 'SC', 1, lambda e, _PS=PS, _tc=tc: e.activation(
                gstg[_tc][:, :], _PS[:, 0:1024], AF.Silu, scale=sc[:, 5:6]))
            sc_silu[(tg, tc)] = cnt['SC']
            W('gpsimd', 'SC', cnt['SC'])
            E_('gpsimd', 'g%d' % tc, 16, lambda e, _tg=tg, _tc=tc: e.dma_start(
                out=d_g[_tg * TCG + _tc * 128: _tg * TCG + (_tc + 1) * 128, :],
                in_=gstg[_tc][:, :]))

    if last_phase < 4:
        W('sync', 'g0', cnt['g0'])
        W('sync', 'g1', cnt['g1'])
        if audit_sink is not None:
            audit_sink['steps'] = steps; audit_sink['SEM'] = SEM; es.close(); return nc
        with es, nc.Block() as block:
            for eng in ("sync", "gpsimd", "vector", "scalar", "tensor"):
                def body(e, _eng=eng):
                    for (tag, fn) in steps:
                        if tag == _eng:
                            fn(e)
                getattr(block, eng)(body)
        return nc
    t_last_ffn2 = cnt['T']
    W('gpsimd', 'T', t_last_ffn2)
    for wi, dw in enumerate([d_qw, d_kw, d_vw, d_ow]):
        for eb in range(8):
            fn = lambda e, _wi=wi, _eb=eb, _dw=dw: e.dma_start(
                out=wbuf[:, _wi * 8192 + _eb * 1024: _wi * 8192 + (_eb + 1) * 1024],
                in_=_dw[_eb * 128:(_eb + 1) * 128, :])
            E_('gpsimd', 'gl', 16, fn)
    gl_w = cnt['gl']
    W('sync', 'g0', cnt['g0'])
    W('sync', 'g1', cnt['g1'])
    W('scalar', 'g0', cnt['g0'])
    W('scalar', 'g1', cnt['g1'])
    W('scalar', 'V', cnt['V'])
    QWO, KWO, VWO, OWO = 0, 8192, 16384, 24576

    E_('gpsimd', 'G', 1, lambda e: e.memset(KV32[:, :], 0.0))
    E_('gpsimd', 'G', 1, lambda e: e.memset(KV16[:, :], 0.0))
    E_('gpsimd', 'G', 1, lambda e: e.memset(ks32[:, :], 0.0))
    E_('gpsimd', 'G', 1, lambda e: e.memset(ks16[:, :], 0.0))
    g_state0 = cnt['G']

    if p4_mc is not None:
        _NMC_RUN = p4_mc
    else:
        _NMC_RUN = NMC
    v_state = 0
    g_state = g_state0
    sc_out_prev = {}
    v_ab_done = {}
    sc_sevac_d = {}
    v_div_d = {}
    v_a1_d = {}
    for mc in range(_NMC_RUN):
        W('tensor', 'gl', gl_w)
        if mc == 0:
            W('tensor', 'SC', cnt['SC'])
            W('tensor', 'V', cnt['V'])
        elif P4_SER & 2:
            W('tensor', 'SC', cnt['SC'])
            W('tensor', 'V', cnt['V'])
        else:
            W('tensor', 'SC', sc_sevac_d.get(2 * mc - 1, 0))
            W('tensor', 'V', v_div_d.get(2 * mc - 1, 0))
        for which, (WO, dst, bia) in enumerate([(QWO, QT, qbT), (KWO, KT, kbT)]):
            for b in range(8):
                PS = PSA if b < 4 else PSB
                off = (b % 4) * 256
                for kb in range(8):
                    fn = lambda e, _PS=PS, _off=off, _WO=WO, _b=b, _kb=kb, _mc=mc: e.matmul(
                        _PS[:, _off:_off + MC],
                        wbuf[:, _WO + _kb * 1024 + _b * 128: _WO + _kb * 1024 + (_b + 1) * 128],
                        xnT[:, _kb * S + _mc * MC: _kb * S + (_mc + 1) * MC],
                        start=(_kb == 0), stop=(_kb == 7))
                    if kb < 7:
                        Eq('tensor', fn)
                    else:
                        E_('tensor', 'T', 1, fn)
            t_qk = cnt['T']
            W('scalar', 'T', t_qk)
            for b in range(8):
                PS = PSA if b < 4 else PSB
                off = (b % 4) * 256
                fn = lambda e, _PS=PS, _off=off, _b=b, _dst=dst, _bia=bia: e.activation(
                    _dst[:, _b * MC:(_b + 1) * MC], _PS[:, _off:_off + MC], AF.Identity,
                    bias=_bia[:, _b:_b + 1])
                if b < 7:
                    Eq('scalar', fn)
                else:
                    E_('scalar', 'SC', 1, fn)
            sc_qk = cnt['SC']
            W('tensor', 'SC', sc_qk)
        for b in range(8):
            fn = lambda e, _b=b: e.activation(ksq[:, _b * MC:(_b + 1) * MC],
                                              KT[:, _b * MC:(_b + 1) * MC], AF.Square)
            if b < 7:
                Eq('scalar', fn)
            else:
                E_('scalar', 'SC', 1, fn)
        sc_ksq = cnt['SC']
        for tc in range(2):
            PS = PSA if tc == 0 else PSB
            for hf in range(2):
                for kb in range(8):
                    Eq('tensor', lambda e, _PS=PS, _hf=hf, _kb=kb, _mc=mc, _tc=tc: e.matmul(
                        _PS[:, _hf * 512:(_hf + 1) * 512],
                        xnT[:, _kb * S + (_mc * 2 + _tc) * 128: _kb * S + (_mc * 2 + _tc + 1) * 128],
                        wbuf[:, VWO + _kb * 1024 + _hf * 512: VWO + _kb * 1024 + (_hf + 1) * 512],
                        start=(_kb == 0), stop=False))
                fn = lambda e, _PS=PS, _hf=hf: e.matmul(
                    _PS[:, _hf * 512:(_hf + 1) * 512], ones16[0:1, 0:128],
                    rows16[0:1, _hf * 512:(_hf + 1) * 512], start=False, stop=True)
                if hf == 1 and tc == 1:
                    E_('tensor', 'T', 1, fn)
                else:
                    Eq('tensor', fn)
        t_v = cnt['T']
        W('scalar', 'T', t_v)
        for tc in range(2):
            PS = PSA if tc == 0 else PSB
            fn = lambda e, _PS=PS, _tc=tc: e.activation(Vsb[:, _tc * E:(_tc + 1) * E],
                                                        _PS[:, 0:1024], AF.Copy)
            if tc == 1:
                E_('scalar', 'SC', 1, fn)
            else:
                Eq('scalar', fn)
        sc_v = cnt['SC']
        if p4_cut == 'qkv':
            break

        for tci in range(2):
            ci = mc * 2 + tci
            tok0 = tci * 128
            W('tensor', 'SC', sc_v)
            if P4_SER & 1:
                W('tensor', 'SC', cnt['SC'])
                W('tensor', 'V', cnt['V'])
            elif ci >= 1:
                W('tensor', 'SC', sc_sevac_d.get(ci - 1, 0))
                W('tensor', 'V', v_div_d.get(ci - 1, 0))
            for h in range(16):
                po = 64 * (h % 2)
                co = (h // 2) * MC + tok0
                fn = lambda e, _po=po, _co=co, _h=h: e.matmul(
                    PSD[:, 256 + _h:256 + _h + 1], ksq[_po:_po + 64, _co:_co + 128],
                    c2d[_po:_po + 64, 0:1], start=True, stop=True)
                if h < 15:
                    Eq('tensor', fn)
                else:
                    E_('tensor', 'T', 1, fn)
            t_ck = cnt['T']
            W('vector', 'T', t_ck)
            E_('vector', 'V', 1, lambda e: e.tensor_scalar(
                kpbias[:, :], PSD[:, 256:272], -1.0, -LN16, OP.mult, OP.add))
            v_kpb = cnt['V']
            W('scalar', 'T', t_ck)
            W('scalar', 'V', v_kpb)
            E_('scalar', 'SC', 1, lambda e: e.activation(eckb[:, :], PSD[:, 256:272], AF.Exp, scale=-1.0))
            sc_eck = cnt['SC']

            W('tensor', 'G', g_state)
            W('tensor', 'V', v_state)
            W('tensor', 'SC', sc_eck)
            W('tensor', 'V', v_kpb)
            sc_exp_h = {}
            v_am_h = {}
            t_maps_h = {}
            t_head_h = {}
            v_fold_last = 0
            for it in range(18):
                if it >= 5 and (it - 1) % 4 == 0:
                    g0 = it - 5
                    W('vector', 'T', t_head_h[g0 + 3])
                    if g0 + 4 in sc_exp_h:
                        W('vector', 'SC', sc_exp_h[g0 + 4])
                    Eq('vector', lambda e, _g0=g0: e.tensor_tensor(
                        KV32[:, _g0 * 128:(_g0 + 4) * 128], KV32[:, _g0 * 128:(_g0 + 4) * 128],
                        PSE[:, 0:512], OP.add))
                    Eq('vector', lambda e, _g0=g0: e.tensor_tensor(
                        ks32[:, _g0:_g0 + 4], ks32[:, _g0:_g0 + 4],
                        PSD[:, 288:292], OP.add))
                    E_('vector', 'V', 1, lambda e, _g0=g0: e.tensor_tensor(
                        ks32[:, 16 + _g0:16 + _g0 + 4], ks32[:, 16 + _g0:16 + _g0 + 4],
                        PSB[:, 904:908], OP.add))
                    v_fold_last = cnt['V']
                    W('gpsimd', 'V', v_fold_last)
                    Eq('gpsimd', lambda e, _g0=g0: e.tensor_copy(
                        KV16[:, _g0 * 128:(_g0 + 4) * 128], KV32[:, _g0 * 128:(_g0 + 4) * 128]))
                    Eq('gpsimd', lambda e, _g0=g0: e.tensor_copy(
                        ks16[:, _g0:_g0 + 4], ks32[:, _g0:_g0 + 4]))
                    E_('gpsimd', 'G', 1, lambda e, _g0=g0: e.tensor_copy(
                        ks16[:, 16 + _g0:16 + _g0 + 4], ks32[:, 16 + _g0:16 + _g0 + 4]))
                if it <= 15:
                    h = it
                    hb = h % 2
                    po = 64 * (h % 2)
                    co = (h // 2) * MC + tok0
                    if hb == 0:
                        MQP, mqo, MTP, mto = PSC, 0, PSD, 0
                    else:
                        MQP, mqo, MTP, mto = PSB, 0, PSB, 512
                    if P4_SER & 4:
                        W('tensor', 'SC', cnt['SC'])
                        W('tensor', 'V', cnt['V'])
                    elif h >= 2:
                        W('tensor', 'SC', sc_exp_h[h - 2])
                        W('tensor', 'V', max(v_am_h[h - 2], v_fold_last))
                    for mb in range(2):
                        Eq('tensor', lambda e, _mb=mb, _po=po, _co=co, _P=MQP, _o=mqo: e.matmul(
                            _P[:, _o + _mb * 128:_o + (_mb + 1) * 128], projs[_po:_po + 64, _mb * 128:(_mb + 1) * 128],
                            QT[_po:_po + 64, _co:_co + 128], start=True, stop=True))
                    for mb in range(2):
                        Eq('tensor', lambda e, _mb=mb, _po=po, _co=co, _P=MQP, _o=mqo: e.matmul(
                            _P[:, _o + 256 + _mb * 128:_o + 256 + (_mb + 1) * 128], projs[_po:_po + 64, _mb * 128:(_mb + 1) * 128],
                            KT[_po:_po + 64, _co:_co + 128], start=True, stop=True))
                    E_('tensor', 'T', 1, lambda e, _po=po, _co=co, _P=MTP, _o=mto: e.matmul(
                        _P[:, _o:_o + 256], KT[_po:_po + 64, _co:_co + 128], projs[_po:_po + 64, 0:256],
                        start=True, stop=True))
                    t_maps_h[h] = cnt['T']
                    W('scalar', 'T', t_maps_h[h])
                    Eq('scalar', lambda e, _hb=hb, _P=MQP, _o=mqo: e.activation(
                        QPb[_hb][:, :], _P[:, _o:_o + 256], AF.Exp))
                    Eq('scalar', lambda e, _hb=hb, _P=MQP, _o=mqo: e.activation(
                        KPb[_hb][:, :], _P[:, _o + 256:_o + 512], AF.Exp, bias=mln16[:, 0:1]))
                    E_('scalar', 'SC', 1, lambda e, _hb=hb, _h=h, _P=MTP, _o=mto: e.activation(
                        kptb[_hb][:, :], _P[:, _o:_o + 256], AF.Exp, bias=kpbias[:, _h:_h + 1]))
                    sc_exp_h[h] = cnt['SC']
                if 1 <= it <= 16:
                    h = it - 1
                    hb = h % 2
                    ABP, ao = (PSD, 320) if hb == 0 else (PSB, 768)
                    W('tensor', 'SC', sc_exp_h[h])
                    Eq('tensor', lambda e, _hb=hb, _P=ABP, _o=ao: e.matmul(
                        _P[:, _o:_o + 128], KPb[_hb][:, 0:128], QPb[_hb][:, 0:128], start=True, stop=False))
                    E_('tensor', 'T', 1, lambda e, _hb=hb, _P=ABP, _o=ao: e.matmul(
                        _P[:, _o:_o + 128], KPb[_hb][:, 128:256], QPb[_hb][:, 128:256], start=False, stop=True))
                    t_A = cnt['T']
                    W('vector', 'T', t_A)
                    W('vector', 'SC', sc_eck)
                    E_('vector', 'V', 1, lambda e, _hb=hb, _h=h, _P=ABP, _o=ao: e.scalar_tensor_tensor(
                        Am16[_hb][:, :], _P[:, _o:_o + 128], eckb[:, _h:_h + 1], maskT[:, :],
                        OP.mult, OP.mult))
                    v_am_h[h] = cnt['V']
                    W('tensor', 'V', v_am_h[h])
                    Eq('tensor', lambda e, _hb=hb, _h=h, _tci=tci: e.matmul(
                        PSA[:, _h * 64:(_h + 1) * 64], Am16[_hb][:, :],
                        Vsb[:, _tci * E + _h * 64: _tci * E + (_h + 1) * 64], start=True, stop=False))
                    for b in range(2):
                        Eq('tensor', lambda e, _hb=hb, _h=h, _b=b: e.matmul(
                            PSA[:, _h * 64:(_h + 1) * 64], QPb[_hb][:, _b * 128:(_b + 1) * 128],
                            KV16[:, (2 * _h + _b) * 64:(2 * _h + _b + 1) * 64],
                            start=False, stop=(_b == 1)))
                    DP, dco = (PSD, 272 + (h >> 1)) if hb == 0 else (PSB, 896 + (h >> 1))
                    Eq('tensor', lambda e, _hb=hb, _P=DP, _c=dco: e.matmul(
                        _P[:, _c:_c + 1], Am16[_hb][:, :], ones16[:, 0:1],
                        start=True, stop=False))
                    for b in range(2):
                        kcol = (h >> 1) * 2 + b + 16 * (h & 1)
                        Eq('tensor', lambda e, _hb=hb, _b=b, _P=DP, _c=dco, _kc=kcol: e.matmul(
                            _P[:, _c:_c + 1], QPb[_hb][:, _b * 128:(_b + 1) * 128],
                            ks16[:, _kc:_kc + 1], start=False, stop=(_b == 1)))
                    for b in range(2):
                        Eq('tensor', lambda e, _hb=hb, _h=h, _b=b, _tci=tci: e.matmul(
                            PSE[:, (_h % 4) * 128 + _b * 64:(_h % 4) * 128 + (_b + 1) * 64],
                            kptb[_hb][:, _b * 128:(_b + 1) * 128],
                            Vsb[:, _tci * E + _h * 64: _tci * E + (_h + 1) * 64],
                            start=True, stop=True))
                    KP_, kco = (PSD, 288 + ((h % 4) >> 1) * 2) if hb == 0 else (PSB, 904 + ((h % 4) >> 1) * 2)
                    for b in range(2):
                        fn = lambda e, _hb=hb, _b=b, _P=KP_, _c=kco: e.matmul(
                            _P[:, _c + _b:_c + _b + 1],
                            kptb[_hb][:, _b * 128:(_b + 1) * 128], ones16[:, 0:1],
                            start=True, stop=True)
                        if b < 1:
                            Eq('tensor', fn)
                        else:
                            E_('tensor', 'T', 1, fn)
                    t_head_h[h] = cnt['T']
            g_state = cnt['G']
            v_state = cnt['V']
            if p4_cut == 'heads':
                break

            W('vector', 'T')
            Eq('vector', lambda e: e.reciprocal(recipb[:, 0:8], PSD[:, 272:280]))
            E_('vector', 'V', 1, lambda e: e.reciprocal(recipb[:, 8:16], PSB[:, 896:904]))
            for h in range(16):
                rc = (h >> 1) + 8 * (h & 1)
                fn = lambda e, _h=h, _rc=rc: e.tensor_scalar(
                    atok[:, _h * 64:(_h + 1) * 64], PSA[:, _h * 64:(_h + 1) * 64],
                    recipb[:, _rc:_rc + 1], None, OP.mult)
                if h < 15:
                    Eq('vector', fn)
                else:
                    E_('vector', 'V', 1, fn)
            v_div = cnt['V']
            v_div_d[ci] = v_div
            if p4_cut == 'div':
                break
            W('tensor', 'V', v_div)
            if ci >= 1:
                W('tensor', 'SC', sc_out_prev.get(ci - 1, 0))
            for b in range(8):
                fn = lambda e, _b=b: e.matmul(
                    PST[:, _b * 128:(_b + 1) * 128], atok[:, _b * 128:(_b + 1) * 128],
                    idn16[:, :], is_transpose=True, start=True, stop=True)
                if b < 7:
                    Eq('tensor', fn)
                else:
                    E_('tensor', 'T', 1, fn)
            t_att = cnt['T']
            W('scalar', 'T', t_att)
            E_('scalar', 'SC', 1, lambda e: e.activation(atT[:, :], PST[:, 0:1024], AF.Copy))
            sc_att = cnt['SC']
            sc_out_prev[ci] = sc_att
            if p4_cut == 'tr':
                break
            W('tensor', 'SC', sc_att)
            for hf in range(2):
                for kb in range(8):
                    Eq('tensor', lambda e, _hf=hf, _kb=kb: e.matmul(
                        PSB[:, _hf * 512:(_hf + 1) * 512], atT[:, _kb * 128:(_kb + 1) * 128],
                        wbuf[:, OWO + _kb * 1024 + _hf * 512: OWO + _kb * 1024 + (_hf + 1) * 512],
                        start=(_kb == 0), stop=False))
                Eq('tensor', lambda e, _hf=hf: e.matmul(
                    PSB[:, _hf * 512:(_hf + 1) * 512], ones16[32:33, 0:128],
                    rows16[32:33, _hf * 512:(_hf + 1) * 512], start=False, stop=False))
                for j in range(4):
                    b = hf * 4 + j
                    fn = lambda e, _hf=hf, _j=j, _b=b, _ci=ci: e.matmul(
                        PSB[:, _hf * 512 + _j * 128:_hf * 512 + (_j + 1) * 128],
                        xnT[:, _b * S + _ci * CH: _b * S + (_ci + 1) * CH],
                        idn16[:, :], start=False, stop=(_j == 3))
                    if hf == 1 and j == 3:
                        E_('tensor', 'T', 1, fn)
                    else:
                        Eq('tensor', fn)
            t_proj = cnt['T']
            if p4_cut == 'proj':
                break
            W('scalar', 'T', t_proj)
            if ci >= 1:
                W('scalar', 'V', v_a1_d.get(ci - 1, 0))
            Eq('scalar', lambda e: e.activation(stok_sb[:, :], PSB[:, 0:1024], AF.Copy))
            E_('scalar', 'SC', 1, lambda e: e.activation(sq2[:, :], stok_sb[:, :], AF.Square))
            sc_sq2 = cnt['SC']
            sc_sevac_d[ci] = sc_sq2
            W('vector', 'SC', sc_sq2)
            Eq('vector', lambda e: e.reduce_sum(ln2s[:, 0:1], stok_sb[:, 0:512], axis=mybir.AxisListType.X))
            Eq('vector', lambda e: e.reduce_sum(ln2s[:, 1:2], stok_sb[:, 512:1024], axis=mybir.AxisListType.X))
            Eq('vector', lambda e: e.tensor_add(ln2s[:, 0:1], ln2s[:, 0:1], ln2s[:, 1:2]))
            Eq('vector', lambda e: e.reduce_sum(ln2s[:, 2:3], sq2[:, :], axis=mybir.AxisListType.X))
            if p4_cut == 'sq':
                mark('V', 1)
                on('vector', lambda e: e.tensor_copy(ln2s[:, 0:1], ln2s[:, 0:1]).then_inc(SEM['V'], 1))
                break
            Eq('vector', lambda e: e.tensor_single_scalar(ln2s[:, 3:4], ln2s[:, 0:1], 1.0 / E, OP.mult))
            Eq('vector', lambda e: e.tensor_single_scalar(ln2s[:, 4:5], ln2s[:, 2:3], 1.0 / E, OP.mult))
            Eq('vector', lambda e: e.tensor_single_scalar(ln2s[:, 5:6], ln2s[:, 3:4], ln2s[:, 3:4], OP.mult))
            Eq('vector', lambda e: e.tensor_sub(ln2s[:, 4:5], ln2s[:, 4:5], ln2s[:, 5:6]))
            Eq('vector', lambda e: e.tensor_single_scalar(ln2s[:, 4:5], ln2s[:, 4:5], LN2_EPS, OP.add))
            E_('vector', 'V', 1, lambda e: e.reciprocal(ln2s[:, 6:7], ln2s[:, 4:5]))
            v_ln2 = cnt['V']
            W('scalar', 'V', v_ln2)
            E_('scalar', 'SC', 1, lambda e: e.activation(ln2s[:, 7:8], ln2s[:, 6:7], AF.Sqrt))
            sc_r2 = cnt['SC']
            if p4_cut == 'ln2':
                break
            if ci >= 2:
                W('sync', 'V', v_ab_done.get(ci - 2, 0))
            _xs4 = 'x0' if ci % 2 == 0 else 'x1'
            E_('sync', _xs4, 16, lambda e, _ci=ci: e.dma_start(
                out=x2t[_ci % 2][:, :], in_=d_x[_ci * CH:(_ci + 1) * CH, :]))
            E_('sync', _xs4, 16, lambda e, _ci=ci: e.dma_start(
                out=g2t[_ci % 2][:, :], in_=d_g[_ci * CH:(_ci + 1) * CH, :]))
            xr_t = cnt[_xs4]
            W('vector', 'SC', sc_r2)
            Eq('vector', lambda e, _ci=ci: e.tensor_scalar(
                ab[_ci % 2][:, :], stok_sb[:, :], ln2s[:, 3:4], ln2s[:, 7:8],
                OP.subtract, OP.mult))
            v_a1_d[ci] = cnt['V']
            if p4_cut == 'a1':
                mark('V', 1)
                on('vector', lambda e: e.tensor_copy(ln2s[:, 0:1], ln2s[:, 0:1]).then_inc(SEM['V'], 1))
                break
            Eq('vector', lambda e, _ci=ci: e.tensor_tensor(ab[_ci % 2][:, :], ab[_ci % 2][:, :], gammab[:, :], OP.mult))
            Eq('vector', lambda e, _ci=ci: e.tensor_tensor(ab[_ci % 2][:, :], ab[_ci % 2][:, :], betab[:, :], OP.add))
            W('vector', _xs4, xr_t)
            Eq('vector', lambda e, _ci=ci: e.tensor_tensor(ab[_ci % 2][:, :], ab[_ci % 2][:, :], g2t[_ci % 2][:, :], OP.mult))
            if p4_cut == 'a4':
                mark('V', 1)
                on('vector', lambda e: e.tensor_copy(ln2s[:, 0:1], ln2s[:, 0:1]).then_inc(SEM['V'], 1))
                break
            W('vector', _xs4, xr_t)
            if ci >= 2:
                W('vector', 'out', 16 * (ci - 1))
            E_('vector', 'V', 1, lambda e, _ci=ci: e.tensor_tensor(
                ab2[_ci % 2][:, :], ab[_ci % 2][:, :], x2t[_ci % 2][:, :], OP.add))
            v_ab = cnt['V']
            v_ab_done[ci] = v_ab
            W('sync', 'V', v_ab)
            E_('sync', 'out', 16, lambda e, _ci=ci: e.dma_start(
                out=d_out[_ci * CH:(_ci + 1) * CH, :], in_=ab2[_ci % 2][:, :]))
        if p4_cut in ('heads', 'div', 'tr', 'proj', 'ln2', 'a1', 'a4', 'sq'):
            break

    if p4_dbg:
        W('vector', 'T', cnt['T'])
        W('vector', 'SC', cnt['SC'])
        Eq('vector', lambda e: e.tensor_copy(st1[:, 0:8], sc[:, 0:8]))
        Eq('vector', lambda e: e.tensor_copy(xb[0][:, 0:12], sc[:, 0:12]))
        E_('vector', 'V', 1, lambda e: e.tensor_copy(xb[0][:, 12:13], cmag[:, 0:1]))
        Eq('vector', lambda e: e.tensor_copy(xb[0][:, 16:32], kpbias[:, 0:16]))
        Eq('vector', lambda e: e.tensor_copy(xb[0][:, 32:48], eckb[:, 0:16]))
        E_('vector', 'V', 1, lambda e: e.tensor_copy(xb[0][:, 48:64], recipb[:, 0:16]))
        W('sync', 'V', cnt['V'])
        E_('sync', 'out', 16, lambda e: e.dma_start(out=d_dbg[:, :], in_=xb[0][:, 0:64]))
    if p4_cut is not None or (p4_mc is not None and p4_mc < NMC):
        W('sync', 'gl', cnt['gl'])
        W('sync', 'T', cnt['T'])
        W('sync', 'V', cnt['V'])
        W('sync', 'SC', cnt['SC'])
        W('sync', 'G', cnt['G'])
    W('sync', 'out', cnt['out'])

    if audit_sink is not None:
        audit_sink['steps'] = steps
        audit_sink['SEM'] = SEM
        es.close()
        return nc
    with es, nc.Block() as block:
        for eng in ("sync", "gpsimd", "vector", "scalar", "tensor"):
            def body(e, _eng=eng):
                for (tag, fn) in steps:
                    if tag == _eng:
                        fn(e)
            getattr(block, eng)(body)

    return nc



def _vap_info(ap):
    t = ap.tensor
    tn = type(t).__name__
    if "SB" not in tn and "PSum" not in tn:
        return None
    dims = [tuple(x) for x in ap.ap]
    pstep, pcnt = dims[0]
    try:
        plo = ap.base_partition()
    except Exception:
        plo = 0
    coloff = ap.offset - plo * pstep if pstep else ap.offset
    span = 0
    for (s, c) in dims[1:]:
        span += s * (c - 1)
    return (t.name, plo, plo + pcnt, coloff, coloff + span + 1)


class _VRec:

    def __init__(self, sink):
        self.sink = sink

    def _aps(self, xs):
        out = []
        for x in xs:
            if hasattr(x, "tensor"):
                i = _vap_info(x)
                if i:
                    out.append(i)
        return out

    def _rec(self, reads, writes):
        self.sink.append((self._aps(reads), self._aps(writes)))
        return self

    def then_inc(self, sem, by=1):
        return self

    def wait_ge(self, sem, v):
        return self

    def reciprocal(self, out, in_):
        return self._rec([in_], [out])

    def tensor_scalar(self, out, in0, s1, s2, op0, op1=None, **kw):
        return self._rec([in0, s1, s2], [out])

    def tensor_single_scalar(self, out, in_, s, op, **kw):
        return self._rec([in_, s], [out])

    def scalar_tensor_tensor(self, out, in0, s, in1, op0, op1, **kw):
        return self._rec([in0, s, in1], [out])

    def tensor_tensor(self, out, a, b, op, **kw):
        return self._rec([a, b], [out])

    def tensor_add(self, out, a, b, **kw):
        return self._rec([a, b], [out])

    def tensor_sub(self, out, a, b, **kw):
        return self._rec([a, b], [out])

    def tensor_max(self, out, a, b, **kw):
        return self._rec([a, b], [out])

    def tensor_copy(self, out, in_, **kw):
        return self._rec([in_], [out])

    def reduce_max(self, out, in_, **kw):
        return self._rec([in_], [out])

    def reduce_sum(self, out, in_, **kw):
        return self._rec([in_], [out])

    def memset(self, ap, c, **kw):
        return self._rec([], [ap])


def _overlap(a, b):
    return (a[0] == b[0] and not (a[4] <= b[3] or b[4] <= a[3]
                                  or a[2] <= b[1] or b[2] <= a[1]))


def compute_vraw(S, ncores, **kw):
    sink = {}
    build_program(S, ncores, audit_sink=sink, **kw)
    recs = []
    rec = _VRec(recs)
    for (eng, fn) in sink["steps"]:
        if eng == "vector":
            fn(rec)
    waits = {}
    writes = []
    guaranteed = 0
    for o, (rds, wrs) in enumerate(recs):
        need = 0
        for r in rds:
            for (w, vc) in writes:
                if vc > need and _overlap(r, w):
                    need = vc
        if need > guaranteed:
            waits[o] = need
            guaranteed = need
        for w in wrs:
            writes.append((w, o + 1))
    return waits


def make_consts():
    idn = np.eye(128).astype(np.float16)
    idnf = np.eye(128).astype(np.float32)
    mT = np.triu(np.ones((128, 128))).astype(np.float16)
    ones16 = np.ones((128, 128), np.float16)
    onesf = np.ones((128, 128), np.float32)
    c2d = np.full((128, 1), 1.0 / (2 * DH), np.float16)
    three = np.full((128, 1), 3.0, np.float32)
    return idn, idnf, mT, ones16, onesf, c2d, three


def kernel(L0_attn_x, L0_attn_q_w, L0_attn_q_b, L0_attn_k_w, L0_attn_k_b,
           L0_attn_v_w, L0_attn_v_b, L0_attn_out_w, L0_attn_out_b,
           L0_attn_proj, L0_attn_gamma, L0_attn_beta,
           L0_ffn_W_in, L0_ffn_b_in, L0_ffn_W_out, L0_ffn_b_out):
    X = np.ascontiguousarray(np.asarray(L0_attn_x, np.float32))
    B, S, _ = X.shape
    assert B == NCORES
    idn, idnf, mT, ones16, onesf, c2d, three = make_consts()
    shared = {
        "q_w": np.ascontiguousarray(np.asarray(L0_attn_q_w, np.float32)),
        "k_w": np.ascontiguousarray(np.asarray(L0_attn_k_w, np.float32)),
        "v_w": np.ascontiguousarray(np.asarray(L0_attn_v_w, np.float32)),
        "out_w": np.ascontiguousarray(np.asarray(L0_attn_out_w, np.float32)),
        "q_b": np.asarray(L0_attn_q_b, np.float32).reshape(1, E),
        "k_b": np.asarray(L0_attn_k_b, np.float32).reshape(1, E),
        "v_b": np.asarray(L0_attn_v_b, np.float32).reshape(1, E),
        "out_b": np.asarray(L0_attn_out_b, np.float32).reshape(1, E),
        "projT": np.ascontiguousarray(np.asarray(L0_attn_proj, np.float32).T),
        "gamma": np.asarray(L0_attn_gamma, np.float32).reshape(1, E),
        "beta": np.asarray(L0_attn_beta, np.float32).reshape(1, E),
        "W_in": np.ascontiguousarray(np.asarray(L0_ffn_W_in, np.float32)),
        "b_in": np.asarray(L0_ffn_b_in, np.float32).reshape(1, FF),
        "W_out": np.ascontiguousarray(np.asarray(L0_ffn_W_out, np.float32)),
        "b_out": np.asarray(L0_ffn_b_out, np.float32).reshape(1, E),
        "idn16": idn, "idnf": idnf, "maskT": mT, "ones16": ones16,
        "onesf": onesf, "c2d": c2d, "three": three,
        "mln16": np.full((128, 1), -LN16, np.float32),
    }
    in_maps = [dict(shared, x=np.ascontiguousarray(X[c])) for c in range(NCORES)]
    vw = compute_vraw(S, NCORES)
    nc = build_program(S, NCORES, vraw_waits=vw)
    res = run_bass_kernel_spmd(nc, in_maps, core_ids=list(range(NCORES)))
    out = np.stack([np.asarray(res.results[c]["out"], np.float32) for c in range(NCORES)])
    return out
